# version 01
# baseline (speedup 1.0000x reference)
import numpy as np
from contextlib import ExitStack
import concourse.bass as bass
import concourse.mybir as mybir
from concourse.bass_utils import run_bass_kernel_spmd
from concourse.expressions import RuntimeValue

F32 = mybir.dt.float32
F32R = mybir.dt.float32r
BF16 = mybir.dt.bfloat16
I32 = mybir.dt.int32
AF = mybir.ActivationFunctionType
ALU = mybir.AluOpType

NCORE = 8
T = 16384
TLOC = T // NCORE
HALO = 128
TEXT = TLOC + HALO
D = 1024
DI = 2048
DS = 16
DTR = 64
ILOC = DI // NCORE
TC = 1024
NCH = T // TC
EPS1 = 1e-5
EPS2 = 1e-6

DEBUG = []
MAXPHASE = 9
P1CUT = 9
SKIP = set()


class Sem:
    def __init__(self, stack, nc, name):
        self.h = stack.enter_context(nc.semaphore(name))
        self.n = 0

    def inc(self, instr, k=1):
        instr.then_inc(self.h, k)
        self.n += k
        return self.n


def build(debug=(), maxphase=9, p1cut=9, skip=()):
    skip = set(skip)
    nc = bass.Bass("TRN2", target_bir_lowering=False, detect_race_conditions=False)
    dbg = set(debug)

    din = lambda name, shp, dt: nc.dram_tensor(name, shp, dt, kind="ExternalInput").ap()
    x_ext = din("x_ext", [TEXT, D], F32)
    cid_in = din("cid_in", [1, 1], I32)
    w1T = din("w1T", [D, 2 * DI], F32)
    conv_diag = din("conv_diag", [16, 4, 128, 128], F32R)
    conv_b = din("conv_b", [128, 16], F32)
    xprojT = din("xprojT", [128, 16 * 96], F32R)
    dtwT = din("dtwT", [DTR, ILOC], F32R)
    dtb = din("dtb", [ILOC, 1], F32)
    a_log = din("a_log", [ILOC, DS], F32)
    d_par = din("d_par", [ILOC, 1], F32)
    woT = din("woT", [DI, D], F32R)
    norm_w = din("norm_w", [128, 8], F32)
    eye_f = din("eye_f", [128, 128], F32)
    eye_b = din("eye_b", [128, 128], BF16)
    eye_r = din("eye_r", [128, 128], F32R)
    sel = din("sel", [DS, DS * 128], F32R)
    wgT_sh = din("wgT_sh", [128, 4096], F32)
    wuT_sh = din("wuT_sh", [128, 4096], F32)
    wdT_sh = din("wdT_sh", [512, D], F32)

    out = nc.dram_tensor("out", [TLOC, D], F32, kind="ExternalOutput").ap()
    dbg_outs = {}
    if "xn" in dbg:
        dbg_outs["xn"] = nc.dram_tensor("dbg_xn", [TLOC, D], F32,
                                        kind="ExternalOutput").ap()
    if "xn2t" in dbg:
        dbg_outs["xn2t"] = nc.dram_tensor("dbg_xn2t", [1024, TEXT], F32,
                                          kind="ExternalOutput").ap()
    if "xconv" in dbg:
        dbg_outs["xconv"] = nc.dram_tensor("dbg_xconv", [DI, TLOC], F32,
                                           kind="ExternalOutput").ap()
    if "xdbc" in dbg:
        dbg_outs["xdbc"] = nc.dram_tensor("dbg_xdbc", [96, TLOC], F32,
                                          kind="ExternalOutput").ap()
    if "y" in dbg:
        dbg_outs["y"] = nc.dram_tensor("dbg_y", [ILOC, T], F32,
                                       kind="ExternalOutput").ap()
    if "x1" in dbg:
        dbg_outs["x1"] = nc.dram_tensor("dbg_x1", [TLOC, D], F32,
                                        kind="ExternalOutput").ap()
    if "sxf" in dbg:
        dbg_outs["sxf"] = nc.dram_tensor("dbg_sxf", [128, 64], F32,
                                         kind="ExternalOutput").ap()
    if "xqt" in dbg:
        dbg_outs["xqt"] = nc.dram_tensor("dbg_xqt", [128, T], BF16,
                                         kind="ExternalOutput").ap()
    if "ffn" in dbg:
        dbg_outs["ffn"] = nc.dram_tensor("dbg_ffn", [TLOC, D], F32,
                                         kind="ExternalOutput").ap()
    if "gu" in dbg:
        dbg_outs["gu"] = nc.dram_tensor("dbg_gu", [TLOC, 4096], F32,
                                        kind="ExternalOutput").ap()

    z_park = nc.dram_tensor("z_park", [DI, TLOC], F32).ap()
    xn_park = nc.dram_tensor("xn_park", [TLOC, D], F32).ap()
    x1_park = nc.dram_tensor("x1_park", [TLOC, D], F32).ap()
    gu_park = nc.dram_tensor("gu_park", [TLOC, 4096], F32).ap()
    g_park = nc.dram_tensor("g_park", [DI, TLOC], F32R).ap()
    cin_xc = nc.dram_tensor("cin_xc", [DI, TLOC], F32R).ap()
    cout_xc = nc.dram_tensor("cout_xc", [NCORE, DI, TLOC], F32R).ap()
    cin_db = nc.dram_tensor("cin_db", [96, TLOC], F32R).ap()
    cout_db = nc.dram_tensor("cout_db", [NCORE, 96, TLOC], F32R).ap()
    cin_y = nc.dram_tensor("cin_y", [ILOC, T], F32).ap()
    xcm_dram = nc.dram_tensor("xcm_dram", [ILOC, T], F32R).ap()
    ym_dram = nc.dram_tensor("ym_dram", [DI, TLOC], F32).ap()
    cout_y = nc.dram_tensor("cout_y", [NCORE, ILOC, T], F32).ap()
    cin_ws = nc.dram_tensor("cin_ws", [1, 4], F32).ap()
    cout_ws = nc.dram_tensor("cout_ws", [1, 4], F32).ap()
    cin_wg = nc.dram_tensor("cin_wg", [128, 4096], BF16).ap()
    cout_wg = nc.dram_tensor("cout_wg", [NCORE, 128, 4096], BF16).ap()
    cin_wu = nc.dram_tensor("cin_wu", [128, 4096], BF16).ap()
    cout_wu = nc.dram_tensor("cout_wu", [NCORE, 128, 4096], BF16).ap()
    cin_wd = nc.dram_tensor("cin_wd", [512, D], BF16).ap()
    cout_wd = nc.dram_tensor("cout_wd", [NCORE, 512, D], BF16).ap()
    rg = [list(range(NCORE))]

    with ExitStack() as st:
        e = st.enter_context

        def sb(name, shp, dt):
            return e(nc.sbuf_tensor(name, shp, dt))

        dtwT_s = sb("dtwT_s", [DTR, ILOC], F32R)
        sel_s = sb("sel_s", [DS, DS * 128], F32R)
        eye_fs = sb("eye_fs", [128, 128], F32)
        eye_bs = sb("eye_bs", [128, 128], BF16)
        eye_rs = sb("eye_rs", [128, 128], F32R)
        aneg = sb("aneg", [128, 2 * DS], F32)
        dpar_s = sb("dpar_s", [128, 2], F32)
        dtbn = sb("dtbn", [128, 2], F32)
        normw_s = sb("normw_s", [128, 8], F32)
        convb_s = sb("convb_s", [128, 16], F32)
        ones_col = sb("ones_col", [128, 1], F32)
        ones_row = sb("ones_row", [1, 128], F32)
        one_t = sb("one_t", [128, 1], F32)
        eps1_t = sb("eps1_t", [128, 1], F32)
        eps2_t = sb("eps2_t", [128, 1], F32)
        hlast = [sb(f"hlast_{it}", [128, DS], F32) for it in range(2)]
        scr = sb("scr", [128, 32], F32)
        vspA = sb("vspA", [128, 512], F32)
        vspB = sb("vspB", [128, 512], F32)
        sdump = sb("sdump", [128, 1024], F32)
        swrep = sb("swrep", [128, 4], F32)
        sxf = sb("sxf", [128, 64], F32)
        scid = sb("scid", [1, 1], I32)

        LD = Sem(st, nc, "LD")
        ST = Sem(st, nc, "ST")
        PE = Sem(st, nc, "PE")
        VE = Sem(st, nc, "VE")
        SE = Sem(st, nc, "SE")
        CC = Sem(st, nc, "CC")
        SEMS = [LD, ST, PE, VE, SE, CC]

        ps0 = e(nc.psum_tensor("ps0", [128, 1024], F32))
        ps1 = e(nc.psum_tensor("ps1", [128, 1024], F32))
        ps2 = e(nc.psum_tensor("ps2", [128, 1024], F32))
        ps3 = e(nc.psum_tensor("ps3", [128, 1024], F32))

        block = e(nc.Block())

        def on(name):
            return getattr(block, name)

        def barrier_vals():
            return [(S, S.n) for S in SEMS]

        def wait_all(eng, bv):
            for S, v in bv:
                if v > 0:
                    eng.wait_ge(S.h, v)

        r_cid = nc.sync.alloc_register("r_cid")

        @on("sync")
        def _(s):
            LD.inc(s.dma_start(out=scid[:], in_=cid_in), 16)
            s.wait_ge(LD.h, LD.n)
            s.load(r_cid, scid[0:1, 0:1])
        cidv = RuntimeValue(r_cid)

        with ExitStack() as stw:
            ew = stw.enter_context

            def sbw(name, shp, dt):
                return ew(nc.sbuf_tensor(name, shp, dt))

            wg_sh = sbw("wg_sh", [128, 4096], F32)
            wu_sh = sbw("wu_sh", [128, 4096], F32)
            wd_sh = [sbw(f"wd_sh{i}", [128, D], F32) for i in range(4)]
            wq_g = sbw("wq_g", [128, 4096], BF16)
            wq_u = sbw("wq_u", [128, 4096], BF16)
            wq_d = [sbw(f"wq_d{i}", [128, D], BF16) for i in range(4)]
            wqi = sbw("wqi", [128, 4096], I32)
            wdump = sbw("wdump", [128, 4096], F32)
            asums = sbw("asums", [128, 8], F32)
            ppart = sbw("ppart", [1, 8], F32)
            arbuf = sbw("arbuf", [1, 4], F32)

            @on("sync")
            def _(s):
                for dst, src in [
                    (wg_sh[:], wgT_sh), (wu_sh[:], wuT_sh),
                    (wd_sh[0][:], wdT_sh[0:128, :]), (wd_sh[1][:], wdT_sh[128:256, :]),
                    (wd_sh[2][:], wdT_sh[256:384, :]), (wd_sh[3][:], wdT_sh[384:512, :]),
                    (sel_s[:], sel), (eye_fs[:], eye_f), (eye_bs[:], eye_b),
                    (eye_rs[:], eye_r), (normw_s[:], norm_w), (convb_s[:], conv_b),
                    (dtwT_s[:], dtwT),
                    (aneg[:, 0:DS], a_log[0:128, :]),
                    (aneg[:, DS:2 * DS], a_log[128:256, :]),
                    (dpar_s[:, 0:1], d_par[0:128, :]),
                    (dpar_s[:, 1:2], d_par[128:256, :]),
                    (dtbn[:, 0:1], dtb[0:128, :]),
                    (dtbn[:, 1:2], dtb[128:256, :]),
                ]:
                    LD.inc(s.dma_start(out=dst, in_=src), 16)
            n_ld_w = LD.n

            @on("vector")
            def _(v):
                li = v.memset(vspA[:], 0.0)
                li = v.memset(ones_col[:], 1.0)
                li = v.memset(ones_row[:], 1.0)
                li = v.memset(one_t[:], 1.0)
                li = v.memset(eps1_t[:], EPS1)
                VE.inc(v.memset(eps2_t[:], EPS2))

            @on("scalar")
            def _(sc):
                sc.wait_ge(LD.h, n_ld_w)
                li = sc.activation(wdump[:], wg_sh[:], AF.Abs, bias=0.0, scale=1.0,
                              accum_out=asums[:, 0:1])
                li = sc.activation(wdump[:], wu_sh[:], AF.Abs, bias=0.0, scale=1.0,
                              accum_out=asums[:, 1:2])
                for i in range(4):
                    li = sc.activation(wdump[:, 0:D], wd_sh[i][:], AF.Abs, bias=0.0,
                                  scale=1.0, accum_out=asums[:, 4 + i:5 + i])
                SE.inc(sc.activation(aneg[:], aneg[:], AF.Exp, bias=0.0, scale=1.0))
            n_se_abs = SE.n

            @on("vector")
            def _(v):
                v.wait_ge(SE.h, n_se_abs)
                li = v.tensor_add(asums[:, 2:3], asums[:, 4:5], asums[:, 5:6])
                li = v.tensor_add(asums[:, 3:4], asums[:, 6:7], asums[:, 7:8])
                v.tensor_max(vspB[:, 0:256], vspA[:, 0:256], vspA[:, 0:256])
                li = v.tensor_add(asums[:, 2:3], asums[:, 2:3], asums[:, 3:4])
                VE.inc(v.tensor_scalar_mul(aneg[:], aneg[:], -1.0))
            n_ve_as = VE.n

            @on("tensor")
            def _(p):
                p.wait_ge(VE.h, n_ve_as)
                PE.inc(p.matmul(ps0[0:1, 0:8], ones_col[:], asums[:, 0:8],
                                start=True, stop=True))
            n_pe_as = PE.n

            @on("scalar")
            def _(sc):
                sc.wait_ge(PE.h, n_pe_as)
                SE.inc(sc.activation(ppart[:], ps0[0:1, 0:8], AF.Copy,
                                     bias=0.0, scale=1.0))
            n_se_pp = SE.n

            @on("sync")
            def _(s):
                s.wait_ge(SE.h, n_se_pp)
                ST.inc(s.dma_start(out=cin_ws, in_=ppart[:, 0:4]), 16)
            n_st_ws = ST.n

            @on("gpsimd")
            def _(g):
                g.wait_ge(ST.h, n_st_ws)
                CC.inc(g.collective_compute("AllReduce", ALU.add, rg,
                                            [cin_ws.opt()], [cout_ws.opt()]))
            n_cc_ar = CC.n

            @on("sync")
            def _(s):
                s.wait_ge(CC.h, n_cc_ar)
                LD.inc(s.dma_start(out=arbuf[:], in_=cout_ws), 16)
            n_ld_ar = LD.n

            @on("vector")
            def _(v):
                v.wait_ge(LD.h, n_ld_ar)
                cnt = 1.0 / (4096.0 * D)
                li = v.tensor_scalar_mul(arbuf[:, 0:3], arbuf[:, 0:3], cnt)
                v.tensor_max(vspB[:, 0:256], vspA[:, 0:256], vspA[:, 0:256])
                li = v.tensor_single_scalar(arbuf[:, 0:3], arbuf[:, 0:3], 1e-5, ALU.max)
                v.tensor_max(vspB[:, 0:256], vspA[:, 0:256], vspA[:, 0:256])
                li = v.tensor_copy(ppart[:, 3:6], arbuf[:, 0:3])
                VE.inc(v.reciprocal(ppart[:, 0:3], arbuf[:, 0:3]))
            n_ve_sw = VE.n

            @on("tensor")
            def _(p):
                p.wait_ge(VE.h, n_ve_sw)
                PE.inc(p.matmul(ps0[:, 8:16], ones_row[0:1, :], ppart[:, 0:8],
                                start=True, stop=True))
            n_pe_sw = PE.n

            @on("scalar")
            def _(sc):
                sc.wait_ge(PE.h, n_pe_sw)
                li = sc.activation(swrep[:, 0:3], ps0[:, 8:11], AF.Copy, bias=0.0, scale=1.0)
                SE.inc(sc.activation(sxf[:, 61:64], ps0[:, 11:14], AF.Copy,
                                     bias=0.0, scale=1.0))
            n_se_sw = SE.n

            @on("scalar")
            def _(sc):
                sc.wait_ge(SE.h, n_se_sw)
                li = sc.activation(wdump[:, 0:256], wdump[:, 256:512], AF.Copy,
                              bias=0.0, scale=1.0)
                li = sc.activation(wg_sh[:], wg_sh[:], AF.Copy, bias=0.0, scale=swrep[:, 0:1])
                li = sc.activation(wu_sh[:], wu_sh[:], AF.Copy, bias=0.0, scale=swrep[:, 1:2])
                for i in range(4):
                    li = sc.activation(wd_sh[i][:], wd_sh[i][:], AF.Copy, bias=0.0,
                                  scale=swrep[:, 2:3])
                SE.inc(li)
            n_se_ws = SE.n

            @on("vector")
            def _(v):
                v.wait_ge(SE.h, n_se_ws)
                li = v.tensor_copy(wqi[:], wg_sh[:])
                li = v.tensor_scalar(wq_g[:], wqi[:], -1.0, 1.0, ALU.max, ALU.min)
                li = v.tensor_copy(wqi[:], wu_sh[:])
                li = v.tensor_scalar(wq_u[:], wqi[:], -1.0, 1.0, ALU.max, ALU.min)
                for i in range(4):
                    li = v.tensor_copy(wqi[:, 0:D], wd_sh[i][:])
                    li = v.tensor_scalar(wq_d[i][:], wqi[:, 0:D], -1.0, 1.0,
                                    ALU.max, ALU.min)
                VE.inc(li)
            n_ve_wq = VE.n

            @on("sync")
            def _(s):
                s.wait_ge(VE.h, n_ve_wq)
                ST.inc(s.dma_start(out=cin_wg, in_=wq_g[:]), 16)
                ST.inc(s.dma_start(out=cin_wu, in_=wq_u[:]), 16)
                for i in range(4):
                    ST.inc(s.dma_start(out=cin_wd[128 * i:128 * (i + 1), :],
                                       in_=wq_d[i][:]), 16)
            n_st_wq = ST.n

            @on("gpsimd")
            def _(g):
                g.wait_ge(ST.h, n_st_wq)
                CC.inc(g.collective_compute("AllGather", ALU.bypass, rg,
                                            [cin_wg.opt()], [cout_wg.opt()]))
                CC.inc(g.collective_compute("AllGather", ALU.bypass, rg,
                                            [cin_wu.opt()], [cout_wu.opt()]))
                CC.inc(g.collective_compute("AllGather", ALU.bypass, rg,
                                            [cin_wd.opt()], [cout_wd.opt()]))
            bw = barrier_vals()

        if maxphase < 1:
            @on("sync")
            def _(s):
                wait_all(s, bw)
                s.wait_ge(ST.h, ST.n)
            return nc, dbg_outs

        with ExitStack() as st1:
            e1 = st1.enter_context

            def sb1(name, shp, dt):
                return e1(nc.sbuf_tensor(name, shp, dt))

            xn2T = [sb1(f"xn2T_{j}", [128, TEXT], F32R) for j in range(8)]
            xprojT_s = sb1("xprojT_s", [128, 16 * 96], F32R)
            xtile = [sb1(f"xtile{i}", [128, D], F32) for i in range(2)]
            xn_t = sb1("xn_t", [128, D], F32)
            xn2_t = sb1("xn2_t", [128, D], F32)
            w1f = [sb1(f"w1f{k}", [128, 256], F32) for k in range(8)]
            w1r = [[sb1(f"w1r{pp}_{k}", [128, 256], F32R) for k in range(8)]
                   for pp in range(2)]
            xs_t = [sb1(f"xs_t{i}", [128, TEXT], F32R) for i in range(2)]
            zevac = sb1("zevac", [128, TLOC], F32)
            diag_t = [sb1(f"diag_t{i}", [128, 512], F32R) for i in range(2)]
            xcv_t = [sb1(f"xcv_t{i}", [128, TLOC], F32R) for i in range(2)]
            xdbc_s = sb1("xdbc_s", [96, TLOC], F32R)

            fst = {k: True for k in ("sync", "vector", "scalar", "tensor")}

            def W1(eng, name):
                if fst[name]:
                    wait_all(eng, bw)
                    fst[name] = False

            @on("sync")
            def _(s):
                W1(s, "sync")
                LD.inc(s.dma_start(out=xprojT_s[:], in_=xprojT), 16)

            NT = TEXT // 128
            g_xtile = [0, 0]
            g_xnst = 0
            g_x2tr = 0
            g_scr = 0
            se_trv = 0
            for tt in range(NT):
                buf = xtile[tt % 2]

                @on("sync")
                def _(s, tt=tt, buf=buf, g=g_xtile[tt % 2]):
                    if g:
                        s.wait_ge(VE.h, g)
                    LD.inc(s.dma_start(out=buf[:],
                                       in_=x_ext[128 * tt:128 * (tt + 1), :]), 16)
                n_ld_x = LD.n

                @on("scalar")
                def _(sc, buf=buf, gs=g_scr):
                    W1(sc, "scalar")
                    sc.wait_ge(LD.h, n_ld_x)
                    if gs:
                        sc.wait_ge(VE.h, gs)
                    li = sc.activation(sdump[:, 0:D], buf[:], AF.Copy, bias=0.0, scale=1.0,
                                  accum_out=scr[:, 0:1])
                    SE.inc(sc.activation(sdump[:, 0:D], buf[:], AF.Square, bias=0.0,
                                         scale=1.0, accum_out=scr[:, 1:2]))
                n_se_st = SE.n

                @on("vector")
                def _(v):
                    W1(v, "vector")
                    v.wait_ge(SE.h, n_se_st)
                    li = v.tensor_scalar_mul(scr[:, 2:3], scr[:, 0:1], -1.0 / D)
                    li = v.tensor_scalar_mul(scr[:, 3:4], scr[:, 1:2], 1.0 / D)
                    li = v.tensor_mul(scr[:, 4:5], scr[:, 2:3], scr[:, 2:3])
                    v.tensor_max(vspB[:, 0:256], vspA[:, 0:256], vspA[:, 0:256])
                    VE.inc(v.tensor_sub(scr[:, 5:6], scr[:, 3:4], scr[:, 4:5]))
                n_ve_v = VE.n

                @on("scalar")
                def _(sc):
                    sc.wait_ge(VE.h, n_ve_v)
                    SE.inc(sc.activation(scr[:, 6:7], scr[:, 5:6], AF.Sqrt,
                                         bias=eps1_t[:], scale=1.0))
                n_se_s1 = SE.n

                @on("vector")
                def _(v):
                    v.wait_ge(SE.h, n_se_s1)
                    li = v.reciprocal(scr[:, 7:8], scr[:, 6:7])
                    v.tensor_max(vspB[:, 0:256], vspA[:, 0:256], vspA[:, 0:256])
                    li = v.tensor_mul(scr[:, 8:9], scr[:, 7:8], scr[:, 7:8])
                    v.tensor_max(vspB[:, 0:256], vspA[:, 0:256], vspA[:, 0:256])
                    VE.inc(v.tensor_mul(scr[:, 9:10], scr[:, 8:9], scr[:, 5:6]))
                n_ve_v2 = VE.n

                @on("scalar")
                def _(sc):
                    sc.wait_ge(VE.h, n_ve_v2)
                    SE.inc(sc.activation(scr[:, 10:11], scr[:, 9:10], AF.Sqrt,
                                         bias=eps2_t[:], scale=1.0))
                n_se_s2 = SE.n

                @on("vector")
                def _(v, buf=buf, tt=tt, gxn=g_xnst, gtr=g_x2tr):
                    v.wait_ge(SE.h, n_se_s2)
                    li = v.reciprocal(scr[:, 11:12], scr[:, 10:11])
                    v.tensor_max(vspB[:, 0:256], vspA[:, 0:256], vspA[:, 0:256])
                    li = v.tensor_mul(scr[:, 12:13], scr[:, 7:8], scr[:, 11:12])
                    v.tensor_max(vspB[:, 0:256], vspA[:, 0:256], vspA[:, 0:256])
                    if gxn:
                        v.wait_ge(ST.h, gxn)
                    li = v.tensor_scalar(xn_t[:], buf[:], scr[:, 2:3], scr[:, 7:8],
                                    ALU.add, ALU.mult)
                    if gtr:
                        v.wait_ge(PE.h, gtr)
                    VE.inc(v.tensor_scalar(xn2_t[:], buf[:], scr[:, 2:3],
                                           scr[:, 12:13], ALU.add, ALU.mult))
                n_ve_xn = VE.n
                g_xtile[tt % 2] = n_ve_xn
                g_scr = n_ve_xn

                @on("sync")
                def _(s, tt=tt):
                    s.wait_ge(VE.h, n_ve_xn)
                    if tt >= 1:
                        ST.inc(s.dma_start(out=xn_park[128 * (tt - 1):128 * tt, :],
                                           in_=xn_t[:]), 16)
                g_xnst = ST.n

                @on("tensor")
                def _(p, tt=tt, gse=se_trv):
                    W1(p, "tensor")
                    p.wait_ge(VE.h, n_ve_xn)
                    if gse:
                        p.wait_ge(SE.h, gse)
                    for j in range(8):
                        li = p.transpose(ps0[:, 128 * j:128 * (j + 1)],
                                    xn2_t[:, 128 * j:128 * (j + 1)], eye_fs[:])
                    PE.inc(li)
                n_pe_tr = PE.n
                g_x2tr = n_pe_tr

                @on("scalar")
                def _(sc, tt=tt):
                    sc.wait_ge(PE.h, n_pe_tr)
                    for j in range(8):
                        li = sc.activation(xn2T[j][:, 128 * tt:128 * (tt + 1)],
                                      ps0[:, 128 * j:128 * (j + 1)],
                                      AF.Copy, bias=0.0, scale=1.0)
                    SE.inc(li)
                se_trv = SE.n

            if "xn" in dbg_outs or "xn2t" in dbg_outs:
                @on("sync")
                def _(s):
                    s.wait_ge(SE.h, se_trv)
                    s.wait_ge(ST.h, ST.n)
                    if "xn" in dbg_outs:
                        ST.inc(s.dma_start(out=dbg_outs["xn"], in_=xn_park), 16)
                    if "xn2t" in dbg_outs:
                        for j in range(8):
                            ST.inc(s.dma_start(
                                out=dbg_outs["xn2t"][128 * j:128 * (j + 1), :],
                                in_=xn2T[j][:].bitcast(F32)), 16)

            if p1cut < 1:
                @on("sync")
                def _(s):
                    for S in SEMS:
                        if S.n:
                            s.wait_ge(S.h, S.n)
                return nc, dbg_outs

            NB_IP = [(0, 512), (512, 512), (1024, 512), (1536, 384), (1920, 256)]
            g_ip = [se_trv, se_trv]
            g_cv = [0, 0]
            g_xs = [0, 0]
            g_xcv = [0, 0]
            g_xcvst = [0, 0]
            g_diag = [0, 0]
            g_z = 0
            ld_diag = [0, 0]
            ve_w1 = {}
            ld_w1 = {}
            pe_w1use = {}

            def w1_load(mb):
                pp = mb % 2

                @on("sync")
                def _(s, mb=mb):
                    if mb >= 1:
                        s.wait_ge(VE.h, ve_w1[mb - 1])
                    for k in range(8):
                        LD.inc(s.dma_start(out=w1f[k][:],
                                           in_=w1T[128 * k:128 * (k + 1),
                                                  256 * mb:256 * (mb + 1)]), 16)
                ld_w1[mb] = LD.n

                @on("vector")
                def _(v, mb=mb, pp=pp):
                    v.wait_ge(LD.h, ld_w1[mb])
                    if mb >= 2:
                        v.wait_ge(PE.h, pe_w1use.get(mb - 2, 0))
                    for k in range(8):
                        li = v.tensor_scalar_mul(w1r[pp][k][:], w1f[k][:],
                                            normw_s[:, k:k + 1])
                    VE.inc(li)
                ve_w1[mb] = VE.n

            w1_load(0)
            w1_load(1)

            n_pe_xdbc = 0
            m_count = 32 if p1cut >= 9 else min(p1cut, 32)
            for m in range(m_count):
                mb = m // 2
                pp = mb % 2
                mm = m % 2
                is_xs = m < 16
                if mm == 0 and 1 <= mb <= 14:
                    w1_load(mb + 1)

                if is_xs and "diag" not in skip:
                    @on("sync")
                    def _(s, m=m, g=g_diag[m % 2]):
                        if g:
                            s.wait_ge(PE.h, g)
                        LD.inc(s.dma_start(
                            out=diag_t[m % 2][:].rearrange("p (k q) -> p k q", q=128),
                            in_=conv_diag[m, :, :, :].transpose([1, 0, 2])),
                            16)
                    ld_diag[m % 2] = LD.n

                if "ip" in skip:
                    continue
                for nbi, (col0, nbw) in enumerate(NB_IP):
                    psh = ps0[:, 0:512] if nbi % 2 == 0 else ps0[:, 512:1024]

                    @on("tensor")
                    def _(p, mb=mb, pp=pp, mm=mm, col0=col0, nbw=nbw, psh=psh,
                          g=g_ip[nbi % 2]):
                        p.wait_ge(VE.h, ve_w1[mb])
                        if g:
                            p.wait_ge(SE.h, g)
                        for k in range(8):
                            li = p.matmul(psh[:, 0:nbw],
                                     w1r[pp][k][:, 128 * mm:128 * (mm + 1)],
                                     xn2T[k][:, col0:col0 + nbw],
                                     start=(k == 0), stop=(k == 7))
                        PE.inc(li)
                    n_pe_ip = PE.n
                    if nbi == len(NB_IP) - 1:
                        pe_w1use[mb] = PE.n

                    @on("scalar")
                    def _(sc, m=m, nbi=nbi, col0=col0, nbw=nbw, psh=psh,
                          is_xs=is_xs, gz=g_z, gxs=g_xs[m % 2]):
                        sc.wait_ge(PE.h, n_pe_ip)
                        if is_xs:
                            if nbi == 0 and gxs:
                                sc.wait_ge(PE.h, gxs)
                            li = sc.activation(xs_t[m % 2][:, col0:col0 + nbw],
                                          psh[:, 0:nbw], AF.Copy, bias=0.0, scale=1.0)
                        else:
                            if nbi == 0 and gz:
                                sc.wait_ge(ST.h, gz)
                            if col0 + nbw > HALO:
                                lo = max(col0, HALO)
                                li = sc.activation(zevac[:, lo - HALO:col0 + nbw - HALO],
                                              psh[:, lo - col0:nbw],
                                              AF.Copy, bias=0.0, scale=1.0)
                        SE.inc(li)
                    g_ip[nbi % 2] = SE.n
                n_se_ip = SE.n

                if not is_xs:
                    @on("sync")
                    def _(s, m=m):
                        s.wait_ge(SE.h, n_se_ip)
                        ST.inc(s.dma_start(
                            out=z_park[128 * (m - 16):128 * (m - 15), :],
                            in_=zevac[:]), 16)
                    g_z = ST.n
                    continue
                if "conv" in skip:
                    continue

                for nb2 in range(4):
                    psh = ps1[:, 0:512] if nb2 % 2 == 0 else ps1[:, 512:1024]

                    @on("tensor")
                    def _(p, m=m, nb2=nb2, psh=psh, g=g_cv[nb2 % 2]):
                        p.wait_ge(SE.h, n_se_ip)
                        p.wait_ge(LD.h, ld_diag[m % 2])
                        if g:
                            p.wait_ge(SE.h, g)
                        for k in range(4):
                            base = HALO + 512 * nb2 - 3 + k
                            li = p.matmul(psh[:], diag_t[m % 2][:, 128 * k:128 * (k + 1)],
                                     xs_t[m % 2][:, base:base + 512],
                                     start=(k == 0), stop=(k == 3))
                        PE.inc(li)
                    n_pe_cv = PE.n
                    if nb2 == 3:
                        g_xs[m % 2] = PE.n
                        g_diag[m % 2] = PE.n

                    @on("scalar")
                    def _(sc, m=m, nb2=nb2, psh=psh, gxc=g_xcv[m % 2],
                          gst=g_xcvst[m % 2]):
                        sc.wait_ge(PE.h, n_pe_cv)
                        if nb2 == 0 and gxc:
                            sc.wait_ge(PE.h, gxc)
                        if nb2 == 0 and gst:
                            sc.wait_ge(ST.h, gst)
                        li = sc.activation(xcv_t[m % 2][:, 512 * nb2:512 * (nb2 + 1)],
                                      psh[:], AF.Silu, bias=convb_s[:, m:m + 1],
                                      scale=1.0)
                        SE.inc(li)
                    g_cv[nb2 % 2] = SE.n
                n_se_cv = SE.n

                @on("sync")
                def _(s, m=m):
                    s.wait_ge(SE.h, n_se_cv)
                    ST.inc(s.dma_start(out=cin_xc[128 * m:128 * (m + 1), :],
                                       in_=xcv_t[m % 2][:]), 16)
                g_xcvst[m % 2] = ST.n

                if "xdbc" not in skip:
                    @on("tensor")
                    def _(p, m=m):
                        p.wait_ge(SE.h, n_se_cv)
                        for nb2 in range(4):
                            tgt = ps2 if nb2 < 2 else ps3
                            pw = tgt[0:96, 512 * (nb2 % 2):512 * (nb2 % 2) + 512]
                            li = p.matmul(pw, xprojT_s[:, 96 * m:96 * (m + 1)],
                                     xcv_t[m % 2][:, 512 * nb2:512 * (nb2 + 1)],
                                     start=(m == 0), stop=(m == 15))
                        PE.inc(li)
                    g_xcv[m % 2] = PE.n
                    n_pe_xdbc = PE.n

            if p1cut < 33 and p1cut != 9:
                @on("sync")
                def _(s):
                    for S in SEMS:
                        if S.n:
                            s.wait_ge(S.h, S.n)
                return nc, dbg_outs

            @on("scalar")
            def _(sc):
                sc.wait_ge(PE.h, n_pe_xdbc)
                for nb2 in range(4):
                    tgt = ps2 if nb2 < 2 else ps3
                    li = sc.activation(xdbc_s[:, 512 * nb2:512 * nb2 + 512],
                                  tgt[0:96, 512 * (nb2 % 2):512 * (nb2 % 2) + 512],
                                  AF.Copy, bias=0.0, scale=1.0)
                SE.inc(li)
            n_se_db = SE.n

            @on("sync")
            def _(s):
                s.wait_ge(SE.h, n_se_db)
                ST.inc(s.dma_start(out=cin_db, in_=xdbc_s[:]), 16)
            n_st_db = ST.n

            @on("gpsimd")
            def _(g):
                g.wait_ge(ST.h, n_st_db)
                CC.inc(g.collective_compute("AllGather", ALU.bypass, rg,
                                            [cin_xc.opt()], [cout_xc.opt()]))
                CC.inc(g.collective_compute("AllGather", ALU.bypass, rg,
                                            [cin_db.opt()], [cout_db.opt()]))
            n_cc_p1 = CC.n

            if "xconv" in dbg_outs:
                @on("sync")
                def _(s):
                    s.wait_ge(CC.h, n_cc_p1)
                    ST.inc(s.dma_start(out=dbg_outs["xconv"],
                                       in_=cin_xc.bitcast(F32)), 16)
            if "xdbc" in dbg_outs:
                @on("sync")
                def _(s):
                    s.wait_ge(CC.h, n_cc_p1)
                    ST.inc(s.dma_start(out=dbg_outs["xdbc"],
                                       in_=cin_db.bitcast(F32)), 16)
            b1 = barrier_vals()

        if maxphase < 2:
            @on("sync")
            def _(s):
                wait_all(s, b1)
                s.wait_ge(ST.h, ST.n)
            return nc, dbg_outs

        with ExitStack() as st2:
            e2 = st2.enter_context

            def sb2(name, shp, dt):
                return e2(nc.sbuf_tensor(name, shp, dt))

            dtw_w = [sb2(f"dtw_w{i}", [DTR, TC], F32R) for i in range(2)]
            b_w = [sb2(f"b_w{i}", [DS, TC], F32R) for i in range(2)]
            c_w = [sb2(f"c_w{i}", [DS, TC], F32R) for i in range(2)]
            xcm = [[sb2(f"xcm{it}_{i}", [128, TC], F32R) for i in range(2)]
                   for it in range(2)]
            spl_a = sb2("spl_a", [128, TC], F32)
            spl_r = sb2("spl_r", [128, TC], F32)
            dcp = [sb2(f"dcp{it}", [128, TC], F32) for it in range(2)]
            du_t = [sb2(f"du{it}", [128, TC], F32) for it in range(2)]
            dab = [[sb2(f"dab{it}_{i}", [128, TC], F32) for i in range(2)]
                   for it in range(2)]
            dub = [[sb2(f"dub{it}_{i}", [128, TC], F32) for i in range(2)]
                   for it in range(2)]
            h_buf = [sb2(f"h_buf{it}", [128, TC], F32) for it in range(2)]
            hcr = [sb2(f"hcr{it}", [128, TC], F32R) for it in range(2)]
            yfull = [sb2(f"yfull{it}", [128, TC], F32) for it in range(2)]

            fst2 = {k: True for k in ("sync", "vector", "scalar", "tensor")}

            def W2(eng, name):
                if fst2[name]:
                    wait_all(eng, b1)
                    fst2[name] = False

            g_xcm = [[0, 0], [0, 0]]
            ld_xcm = [[0, 0], [0, 0]]
            g_dbw = [0, 0]
            ld_dbw = [0, 0]

            @on("sync")
            def _(s):
                W2(s, "sync")
                s.wait_ge(CC.h, n_cc_p1)
                chof = nc.snap(RuntimeValue(r_cid) * ILOC)
                for r in range(NCORE):
                    ST.inc(s.dma_start(
                        out=xcm_dram[:, TLOC * r:TLOC * (r + 1)],
                        in_=cout_xc[r, bass.DynSlice(chof, ILOC), :]), 16)
            n_st_xcslice = ST.n

            def xcm_load(c):
                blk = c // 2
                col0 = (c % 2) * TC
                for it in range(2):
                    @on("sync")
                    def _(s, it=it, c=c, g=g_xcm[it][c % 2]):
                        W2(s, "sync")
                        s.wait_ge(ST.h, n_st_xcslice)
                        if g:
                            s.wait_ge(VE.h, g)
                        LD.inc(s.dma_start(
                            out=xcm[it][c % 2][:],
                            in_=xcm_dram[128 * it:128 * (it + 1),
                                         TC * c:TC * (c + 1)]), 16)
                    ld_xcm[it][c % 2] = LD.n

                blk = c // 2
                col0 = (c % 2) * TC

                @on("sync")
                def _(s, c=c, blk=blk, col0=col0, g=g_dbw[c % 2]):
                    s.wait_ge(CC.h, n_cc_p1)
                    if g:
                        s.wait_ge(PE.h, g)
                    LD.inc(s.dma_start(out=dtw_w[c % 2][:],
                                       in_=cout_db[blk, 0:DTR, col0:col0 + TC]), 16)
                    LD.inc(s.dma_start(out=b_w[c % 2][:],
                                       in_=cout_db[blk, DTR:DTR + DS, col0:col0 + TC]),
                           16)
                    LD.inc(s.dma_start(out=c_w[c % 2][:],
                                       in_=cout_db[blk, DTR + DS:96, col0:col0 + TC]),
                           16)
                ld_dbw[c % 2] = LD.n

            xcm_load(0)

            se_dt_gate = 0
            g_spl = 0
            st_yf = 0
            ve_scan_gate = {}
            pe_ssum_gate = 0
            ve_hc_gate = 0
            ve_dub_gate = 0
            se_dA = {}
            ve_duB = {}
            ve_scan = {}
            ve_hC = {}
            pe_B = {}
            pe_C = {}
            n_ve_stt = 0

            for c in range(NCH):
                if c + 1 < NCH:
                    xcm_load(c + 1)

                for it in range(2):
                    @on("tensor")
                    def _(p, c=c, it=it, g=se_dt_gate):
                        W2(p, "tensor")
                        p.wait_ge(LD.h, ld_dbw[c % 2])
                        if g:
                            p.wait_ge(SE.h, g)
                        for nb in range(2):
                            li = p.matmul(ps3[:, 512 * nb:512 * (nb + 1)],
                                     dtwT_s[:, 128 * it:128 * (it + 1)],
                                     dtw_w[c % 2][:, 512 * nb:512 * (nb + 1)],
                                     start=True, stop=True)
                        PE.inc(li)
                    n_pe_dt = PE.n

                    @on("scalar")
                    def _(sc, c=c, it=it, gs=g_spl):
                        W2(sc, "scalar")
                        sc.wait_ge(PE.h, n_pe_dt)
                        if gs:
                            sc.wait_ge(VE.h, gs)
                        li = sc.activation(spl_a[:], ps3[:], AF.Abs,
                                      bias=dtbn[:, it:it + 1], scale=1.0)
                        li = sc.activation(spl_r[:], ps3[:], AF.Relu,
                                      bias=dtbn[:, it:it + 1], scale=1.0)
                        li = sc.activation(spl_a[:], spl_a[:], AF.Exp, bias=0.0, scale=-1.0)
                        SE.inc(sc.activation(spl_a[:], spl_a[:], AF.Ln,
                                             bias=one_t[:], scale=1.0))
                    se_dt_gate = SE.n
                    n_se_sp = SE.n

                    @on("vector")
                    def _(v, c=c, it=it):
                        W2(v, "vector")
                        v.wait_ge(SE.h, n_se_sp)
                        v.wait_ge(LD.h, ld_xcm[it][c % 2])
                        li = v.tensor_add(dcp[it][:], spl_r[:], spl_a[:])
                        li = v.tensor_single_scalar(dcp[it][:], dcp[it][:], 1.0, ALU.min)
                        VE.inc(v.tensor_mul(du_t[it][:], dcp[it][:],
                                            xcm[it][c % 2][:].bitcast(F32)))
                    n_ve_du = VE.n
                    g_spl = VE.n

                for s_i in range(DS):
                    par = s_i % 2

                    @on("tensor")
                    def _(p, c=c, s_i=s_i, g1=ve_hc_gate, g2=n_ve_stt):
                        if g1:
                            p.wait_ge(VE.h, g1)
                        for nb in range(2):
                            li = p.matmul(ps0[:, 512 * nb:512 * (nb + 1)],
                                     sel_s[:, 128 * s_i:128 * (s_i + 1)],
                                     b_w[c % 2][:, 512 * nb:512 * (nb + 1)],
                                     start=True, stop=True)
                        PE.inc(li)
                    pe_B[s_i] = PE.n

                    for it in range(2):
                        @on("scalar")
                        def _(sc, c=c, s_i=s_i, it=it, par=par,
                              g=ve_scan_gate.get((it, par), 0)):
                            if g:
                                sc.wait_ge(VE.h, g)
                            sc.wait_ge(VE.h, n_ve_du)
                            SE.inc(sc.activation(dab[it][par][:], dcp[it][:], AF.Exp,
                                                 bias=0.0,
                                                 scale=aneg[:, DS * it + s_i:DS * it + s_i + 1]))
                        se_dA[(s_i, it)] = SE.n

                    for it in range(2):
                        @on("vector")
                        def _(v, c=c, s_i=s_i, it=it, par=par,
                              g=ve_scan_gate.get((it, par), 0)):
                            v.wait_ge(PE.h, pe_B[s_i])
                            VE.inc(v.tensor_tensor(dub[it][par][:], du_t[it][:],
                                                   ps0[:], ALU.mult))
                        ve_duB[(s_i, it)] = VE.n
                    ve_dub_gate = VE.n

                    for it in range(2):
                        @on("vector")
                        def _(v, c=c, s_i=s_i, it=it, par=par):
                            v.wait_ge(SE.h, se_dA[(s_i, it)])
                            init = 0.0 if c == 0 else hlast[it][:, s_i:s_i + 1]
                            VE.inc(v.tensor_tensor_scan(h_buf[it][:], dab[it][par][:],
                                                 dub[it][par][:], init,
                                                 ALU.mult, ALU.add))
                        ve_scan[(s_i, it)] = VE.n
                        ve_scan_gate[(it, par)] = VE.n

                    @on("tensor")
                    def _(p, c=c, s_i=s_i):
                        p.wait_ge(VE.h, ve_dub_gate)
                        for nb in range(2):
                            li = p.matmul(ps0[:, 512 * nb:512 * (nb + 1)],
                                     sel_s[:, 128 * s_i:128 * (s_i + 1)],
                                     c_w[c % 2][:, 512 * nb:512 * (nb + 1)],
                                     start=True, stop=True)
                        PE.inc(li)
                    pe_C[s_i] = PE.n

                    for it in range(2):
                        @on("vector")
                        def _(v, s_i=s_i, it=it):
                            v.wait_ge(PE.h, pe_C[s_i])
                            VE.inc(v.tensor_tensor(hcr[it][:], h_buf[it][:],
                                                   ps0[:], ALU.mult))
                        ve_hC[(s_i, it)] = VE.n
                    ve_hc_gate = VE.n

                    @on("vector")
                    def _(v, c=c, s_i=s_i):
                        for it in range(2):
                            li = v.tensor_copy(hlast[it][:, s_i:s_i + 1],
                                               h_buf[it][:, TC - 1:TC])
                        VE.inc(li)

                    for it in range(2):
                        @on("tensor")
                        def _(p, c=c, s_i=s_i, it=it, g=pe_ssum_gate):
                            p.wait_ge(VE.h, ve_hC[(s_i, it)])
                            if s_i == 0 and g:
                                p.wait_ge(VE.h, g)
                            tgt = ps1 if it == 0 else ps2
                            for nb in range(2):
                                li = p.matmul(tgt[:, 512 * nb:512 * (nb + 1)],
                                         eye_rs[:],
                                         hcr[it][:, 512 * nb:512 * (nb + 1)],
                                         start=(s_i == 0), stop=(s_i == DS - 1))
                            PE.inc(li)
                n_pe_ssum = PE.n
                g_dbw[c % 2] = PE.n

                for it in range(2):
                    @on("vector")
                    def _(v, c=c, it=it, gy=st_yf):
                        v.wait_ge(PE.h, n_pe_ssum)
                        if gy:
                            v.wait_ge(ST.h, gy)
                        tgt = ps1 if it == 0 else ps2
                        VE.inc(v.scalar_tensor_tensor(
                            yfull[it][:], xcm[it][c % 2][:].bitcast(F32),
                            dpar_s[:, it:it + 1], tgt[:], ALU.mult, ALU.add))
                    g_xcm[it][c % 2] = VE.n
                n_ve_stt = VE.n
                pe_ssum_gate = VE.n

                for it in range(2):
                    @on("sync")
                    def _(s, c=c, it=it):
                        s.wait_ge(VE.h, n_ve_stt)
                        ST.inc(s.dma_start(
                            out=cin_y[128 * it:128 * (it + 1), TC * c:TC * (c + 1)],
                            in_=yfull[it][:]), 16)
                st_yf = ST.n
            n_st_y = ST.n

            @on("gpsimd")
            def _(g):
                g.wait_ge(ST.h, n_st_y)
                CC.inc(g.collective_compute("AllGather", ALU.bypass, rg,
                                            [cin_y.opt()], [cout_y.opt()]))
            n_cc_y = CC.n

            if "y" in dbg_outs:
                @on("sync")
                def _(s):
                    s.wait_ge(CC.h, n_cc_y)
                    ST.inc(s.dma_start(out=dbg_outs["y"], in_=cin_y), 16)
            b2 = barrier_vals()

        if maxphase < 3:
            @on("sync")
            def _(s):
                wait_all(s, b2)
                s.wait_ge(ST.h, ST.n)
            return nc, dbg_outs

        with ExitStack() as st3:
            e3 = st3.enter_context

            def sb3(name, shp, dt):
                return e3(nc.sbuf_tensor(name, shp, dt))

            wo_s = sb3("wo_s", [128, 16 * D], F32R)
            y_t = [sb3(f"y_t{i}", [128, TLOC], F32) for i in range(2)]
            z_t = [sb3(f"z_t{i}", [128, TLOC], F32) for i in range(2)]
            silu_t = sb3("silu_t", [128, TLOC], F32)
            g_t = sb3("g_t", [128, TLOC], F32R)
            g_in = [sb3(f"g_in{i}", [128, 512], F32R) for i in range(2)]
            mo_nb = [sb3(f"mo_nb{i}", [128, 512], F32) for i in range(8)]
            mot_t = sb3("mot_t", [128, D], F32)
            mop_t = sb3("mop_t", [128, D], F32)
            xn3_t = sb3("xn3_t", [128, D], F32)
            x3_t = sb3("x3_t", [128, D], F32)
            x1_t = sb3("x1_t", [128, D], F32)

            fst3 = {k: True for k in ("sync", "vector", "scalar", "tensor")}

            def W3(eng, name):
                if fst3[name]:
                    wait_all(eng, b2)
                    fst3[name] = False

            @on("sync")
            def _(s):
                W3(s, "sync")
                for k in range(16):
                    LD.inc(s.dma_start(out=wo_s[:, D * k:D * (k + 1)],
                                       in_=woT[128 * k:128 * (k + 1), :]), 16)

            @on("sync")
            def _(s):
                W3(s, "sync")
                s.wait_ge(CC.h, n_cc_y)
                tkof = nc.snap(RuntimeValue(r_cid) * TLOC)
                for r in range(NCORE):
                    ST.inc(s.dma_start(
                        out=ym_dram[ILOC * r:ILOC * (r + 1), :],
                        in_=cout_y[r, :, bass.DynSlice(tkof, TLOC)]), 16)
            n_st_yslice = ST.n

            g_yz = [0, 0]
            g_silu = 0
            g_gst = 0
            pe_gmm = {}
            tt_gate_tr = 0
            st_out = 0
            for m in range(16):

                @on("sync")
                def _(s, m=m, g=g_yz[m % 2]):
                    W3(s, "sync")
                    s.wait_ge(ST.h, n_st_yslice)
                    if g:
                        s.wait_ge(VE.h, g)
                    LD.inc(s.dma_start(out=y_t[m % 2][:],
                                       in_=ym_dram[128 * m:128 * (m + 1), :]), 16)
                    LD.inc(s.dma_start(out=z_t[m % 2][:],
                                       in_=z_park[128 * m:128 * (m + 1), :]), 16)
                n_ld_yz = LD.n

                @on("scalar")
                def _(sc, m=m):
                    W3(sc, "scalar")
                    sc.wait_ge(LD.h, n_ld_yz)
                    if m >= 1:
                        sc.wait_ge(VE.h, g_silu)
                    SE.inc(sc.activation(silu_t[:], z_t[m % 2][:], AF.Silu,
                                         bias=0.0, scale=1.0))
                n_se_silu = SE.n

                @on("vector")
                def _(v, m=m):
                    W3(v, "vector")
                    v.wait_ge(SE.h, n_se_silu)
                    if m >= 1:
                        v.wait_ge(ST.h, g_gst)
                    VE.inc(v.tensor_mul(g_t[:], y_t[m % 2][:], silu_t[:]))
                g_yz[m % 2] = VE.n
                g_silu = VE.n

                @on("sync")
                def _(s, m=m):
                    s.wait_ge(VE.h, g_silu)
                    ST.inc(s.dma_start(out=g_park[128 * m:128 * (m + 1), :],
                                       in_=g_t[:]), 16)
                g_gst = ST.n
            n_st_g = ST.n

            se_monb = 0
            for nb in range(4):
                ld_g = {}
                for k in range(16):
                    @on("sync")
                    def _(s, nb=nb, k=k):
                        s.wait_ge(ST.h, n_st_g)
                        gk = pe_gmm.get((nb, k - 2)) if k >= 2 else (
                            pe_gmm.get((nb - 1, 14 + k)) if nb >= 1 else None)
                        if gk:
                            s.wait_ge(PE.h, gk)
                        LD.inc(s.dma_start(
                            out=g_in[k % 2][:],
                            in_=g_park[128 * k:128 * (k + 1),
                                       512 * nb:512 * (nb + 1)]), 16)
                    ld_g[k] = LD.n

                    @on("tensor")
                    def _(p, nb=nb, k=k, g=tt_gate_tr):
                        W3(p, "tensor")
                        p.wait_ge(LD.h, ld_g[k])
                        if k == 0 and g:
                            p.wait_ge(SE.h, g)
                        for mc in range(8):
                            tgt = [ps0, ps1, ps2, ps3][mc // 2]
                            ph = tgt[:, 512 * (mc % 2):512 * (mc % 2) + 512]
                            li = p.matmul(ph, wo_s[:, D * k + 128 * mc:D * k + 128 * (mc + 1)],
                                     g_in[k % 2][:], start=(k == 0), stop=(k == 15))
                        PE.inc(li)
                    pe_gmm[(nb, k)] = PE.n
                n_pe_mo = PE.n

                @on("scalar")
                def _(sc, nb=nb):
                    sc.wait_ge(PE.h, n_pe_mo)
                    for mc in range(8):
                        tgt = [ps0, ps1, ps2, ps3][mc // 2]
                        ph = tgt[:, 512 * (mc % 2):512 * (mc % 2) + 512]
                        li = sc.activation(mo_nb[mc][:], ph, AF.Copy, bias=0.0, scale=1.0)
                    SE.inc(li)
                se_monb = SE.n

                for sub in range(4):
                    tt = nb * 4 + sub

                    @on("tensor")
                    def _(p, nb=nb, sub=sub, g=se_monb, g2=tt_gate_tr):
                        p.wait_ge(SE.h, g)
                        if g2:
                            p.wait_ge(SE.h, g2)
                        for mc in range(8):
                            li = p.transpose(ps0[:, 128 * mc:128 * (mc + 1)],
                                        mo_nb[mc][:, 128 * sub:128 * (sub + 1)],
                                        eye_fs[:])
                        PE.inc(li)
                    n_pe_mtr = PE.n

                    @on("scalar")
                    def _(sc):
                        sc.wait_ge(PE.h, n_pe_mtr)
                        for mc in range(8):
                            li = sc.activation(mot_t[:, 128 * mc:128 * (mc + 1)],
                                          ps0[:, 128 * mc:128 * (mc + 1)],
                                          AF.Copy, bias=0.0, scale=1.0)
                        SE.inc(li)
                    tt_gate_tr = SE.n
                    n_se_mot = SE.n

                    @on("sync")
                    def _(s, tt=tt):
                        LD.inc(s.dma_start(out=xn3_t[:],
                                           in_=xn_park[128 * tt:128 * (tt + 1), :]), 16)
                        LD.inc(s.dma_start(
                            out=x3_t[:],
                            in_=x_ext[HALO + 128 * tt:HALO + 128 * (tt + 1), :]), 16)
                    n_ld_xn3 = LD.n

                    @on("vector")
                    def _(v):
                        v.wait_ge(SE.h, n_se_mot)
                        v.wait_ge(LD.h, n_ld_xn3)
                        VE.inc(v.tensor_add(mop_t[:], mot_t[:], xn3_t[:]))
                    n_ve_mop = VE.n

                    @on("scalar")
                    def _(sc):
                        sc.wait_ge(VE.h, n_ve_mop)
                        li = sc.activation(sdump[:, 0:D], mop_t[:], AF.Copy, bias=0.0,
                                      scale=1.0, accum_out=scr[:, 0:1])
                        SE.inc(sc.activation(sdump[:, 0:D], mop_t[:], AF.Square,
                                             bias=0.0, scale=1.0,
                                             accum_out=scr[:, 1:2]))
                    n_se_st3 = SE.n

                    @on("vector")
                    def _(v):
                        v.wait_ge(SE.h, n_se_st3)
                        li = v.tensor_scalar_mul(scr[:, 2:3], scr[:, 0:1], -1.0 / D)
                        li = v.tensor_scalar_mul(scr[:, 3:4], scr[:, 1:2], 1.0 / D)
                        li = v.tensor_mul(scr[:, 4:5], scr[:, 2:3], scr[:, 2:3])
                        v.tensor_max(vspB[:, 0:256], vspA[:, 0:256], vspA[:, 0:256])
                        VE.inc(v.tensor_sub(scr[:, 5:6], scr[:, 3:4], scr[:, 4:5]))
                    n_ve_v3 = VE.n

                    @on("scalar")
                    def _(sc):
                        sc.wait_ge(VE.h, n_ve_v3)
                        SE.inc(sc.activation(scr[:, 6:7], scr[:, 5:6], AF.Sqrt,
                                             bias=eps1_t[:], scale=1.0))
                    n_se_sq3 = SE.n

                    @on("vector")
                    def _(v, tt=tt, g=st_out):
                        v.wait_ge(SE.h, n_se_sq3)
                        li = v.reciprocal(scr[:, 7:8], scr[:, 6:7])
                        v.tensor_max(vspB[:, 0:256], vspA[:, 0:256], vspA[:, 0:256])
                        li = v.tensor_mul(scr[:, 8:9], scr[:, 2:3], scr[:, 7:8])
                        if g:
                            v.wait_ge(ST.h, g)
                        v.tensor_max(vspB[:, 0:256], vspA[:, 0:256], vspA[:, 0:256])
                        li = v.tensor_scalar(x1_t[:], mop_t[:], scr[:, 7:8], scr[:, 8:9],
                                        ALU.mult, ALU.add)
                        VE.inc(v.tensor_add(x1_t[:], x1_t[:], x3_t[:]))
                    n_ve_x1 = VE.n

                    @on("sync")
                    def _(s, tt=tt):
                        s.wait_ge(VE.h, n_ve_x1)
                        ST.inc(s.dma_start(out=x1_park[128 * tt:128 * (tt + 1), :],
                                           in_=x1_t[:]), 16)
                    st_out = ST.n
            n_st_x1 = ST.n

            if "x1" in dbg_outs:
                @on("sync")
                def _(s):
                    s.wait_ge(ST.h, n_st_x1)
                    ST.inc(s.dma_start(out=dbg_outs["x1"], in_=x1_park), 16)
            b3 = barrier_vals()

        if maxphase < 4:
            @on("sync")
            def _(s):
                wait_all(s, b3)
                s.wait_ge(ST.h, ST.n)
            return nc, dbg_outs

        with ExitStack() as st4:
            e4 = st4.enter_context

            def sb4(name, shp, dt):
                return e4(nc.sbuf_tensor(name, shp, dt))

            xqT = sb4("xqT", [128, T], BF16)
            x4_t = [sb4(f"x4_t{i}", [128, D], F32) for i in range(2)]
            q_i32 = sb4("q_i32", [128, D], I32)
            q_bf = sb4("q_bf", [128, D], F32)
            gumax = sb4("gumax", [128, 16], F32)

            fst4 = {k: True for k in ("sync", "vector", "scalar", "tensor")}

            def W4(eng, name):
                if fst4[name]:
                    wait_all(eng, b3)
                    fst4[name] = False

            x4n = sb4("x4n", [128, D], F32)
            g_x4 = [0, 0]
            g_qtr = 0
            g_sdq = 0
            g_qbf = 0
            for tt in range(16):
                @on("sync")
                def _(s, tt=tt, g=g_x4[tt % 2]):
                    W4(s, "sync")
                    if g:
                        s.wait_ge(VE.h, g)
                    LD.inc(s.dma_start(out=x4_t[tt % 2][:],
                                       in_=x1_park[128 * tt:128 * (tt + 1), :]), 16)
                n_ld_x4 = LD.n

                @on("scalar")
                def _(sc, tt=tt):
                    W4(sc, "scalar")
                    sc.wait_ge(LD.h, n_ld_x4)
                    li = sc.activation(sdump[:, 0:D], x4_t[tt % 2][:], AF.Copy,
                                  bias=0.0, scale=1.0, accum_out=scr[:, 4:5])
                    SE.inc(sc.activation(sdump[:, 0:D], x4_t[tt % 2][:], AF.Square,
                                         bias=0.0, scale=1.0,
                                         accum_out=scr[:, 5:6]))
                n_se_st4p = SE.n

                @on("vector")
                def _(v, tt=tt):
                    W4(v, "vector")
                    v.wait_ge(SE.h, n_se_st4p)
                    li = v.tensor_scalar_mul(scr[:, 6:7], scr[:, 4:5], -1.0 / D)
                    li = v.tensor_scalar_mul(scr[:, 7:8], scr[:, 5:6], 1.0 / D)
                    li = v.tensor_mul(scr[:, 8:9], scr[:, 6:7], scr[:, 6:7])
                    v.tensor_max(vspB[:, 0:256], vspA[:, 0:256], vspA[:, 0:256])
                    VE.inc(v.tensor_sub(scr[:, 9:10], scr[:, 7:8], scr[:, 8:9]))
                n_ve_var4 = VE.n

                @on("scalar")
                def _(sc, tt=tt):
                    sc.wait_ge(VE.h, n_ve_var4)
                    SE.inc(sc.activation(scr[:, 10:11], scr[:, 9:10], AF.Sqrt,
                                         bias=eps1_t[:], scale=1.0))
                n_se_sq4p = SE.n

                @on("vector")
                def _(v, tt=tt):
                    v.wait_ge(SE.h, n_se_sq4p)
                    li = v.reciprocal(scr[:, 11:12], scr[:, 10:11])
                    v.tensor_max(vspB[:, 0:256], vspA[:, 0:256], vspA[:, 0:256])
                    li = v.tensor_mul(scr[:, 12:13], scr[:, 6:7], scr[:, 11:12])
                    v.tensor_max(vspB[:, 0:256], vspA[:, 0:256], vspA[:, 0:256])
                    li = v.tensor_scalar(x4n[:], x4_t[tt % 2][:], scr[:, 11:12],
                                    scr[:, 12:13], ALU.mult, ALU.add)
                    li = v.tensor_reduce(scr[:, 0:1], x4n[:], mybir.AxisListType.X,
                                    ALU.max, apply_absolute_value=True)
                    li = v.tensor_max(vspB[:], vspA[:], vspA[:])
                    li = v.tensor_single_scalar(scr[:, 0:1], scr[:, 0:1], 1e-5, ALU.max)
                    v.tensor_max(vspB[:, 0:256], vspA[:, 0:256], vspA[:, 0:256])
                    li = v.reciprocal(scr[:, 1:2], scr[:, 0:1])
                    v.tensor_max(vspB[:, 0:256], vspA[:, 0:256], vspA[:, 0:256])
                    li = v.tensor_scalar_mul(sxf[:, tt:tt + 1], scr[:, 1:2], 127.0)
                    li = v.tensor_scalar_mul(scr[:, 2:3], scr[:, 0:1], 1.0 / 127.0)
                    v.tensor_max(vspB[:, 0:256], vspA[:, 0:256], vspA[:, 0:256])
                    li = v.tensor_mul(sxf[:, 16 + tt:17 + tt], scr[:, 2:3], sxf[:, 61:62])
                    VE.inc(v.tensor_mul(sxf[:, 32 + tt:33 + tt], scr[:, 2:3],
                                        sxf[:, 62:63]))
                n_ve_aq = VE.n
                g_x4[tt % 2] = VE.n

                @on("scalar")
                def _(sc, tt=tt, g=g_sdq):
                    sc.wait_ge(VE.h, n_ve_aq)
                    if g:
                        sc.wait_ge(VE.h, g)
                    SE.inc(sc.activation(sdump[:, 0:D], x4n[:], AF.Copy,
                                         bias=0.0, scale=sxf[:, tt:tt + 1]))
                n_se_q = SE.n

                @on("vector")
                def _(v, tt=tt, g=g_qbf):
                    v.wait_ge(SE.h, n_se_q)
                    if g:
                        v.wait_ge(PE.h, g)
                    li = v.tensor_copy(q_i32[:], sdump[:, 0:D])
                    VE.inc(v.tensor_copy(q_bf[:], q_i32[:]))
                n_ve_q = VE.n
                g_sdq = VE.n

                @on("tensor")
                def _(p, tt=tt, g=g_qtr):
                    W4(p, "tensor")
                    p.wait_ge(VE.h, n_ve_q)
                    if g:
                        p.wait_ge(SE.h, g)
                    for j in range(8):
                        li = p.transpose(ps3[:, 128 * j:128 * (j + 1)],
                                    q_bf[:, 128 * j:128 * (j + 1)], eye_fs[:])
                    PE.inc(li)
                n_pe_qtr = PE.n
                g_qbf = PE.n

                @on("scalar")
                def _(sc, tt=tt):
                    sc.wait_ge(PE.h, n_pe_qtr)
                    for j in range(8):
                        li = sc.activation(xqT[:, D * tt + 128 * j:D * tt + 128 * (j + 1)],
                                      ps3[:, 128 * j:128 * (j + 1)],
                                      AF.Copy, bias=0.0, scale=1.0)
                    SE.inc(li)
                g_qtr = SE.n
            n_se_xqT = SE.n

            if "xqt" in dbg_outs:
                @on("sync")
                def _(s):
                    s.wait_ge(SE.h, n_se_xqT)
                    ST.inc(s.dma_start(out=dbg_outs["xqt"], in_=xqT[:]), 16)
            if "sxf" in dbg_outs:
                @on("sync")
                def _(s):
                    s.wait_ge(SE.h, n_se_xqT)
                    ST.inc(s.dma_start(out=dbg_outs["sxf"], in_=sxf[:]), 16)

            with ExitStack() as st4a:
                e4a = st4a.enter_context
                wgb = [e4a(nc.sbuf_tensor(f"wgb{i}", [128, 8 * 1024], BF16))
                       for i in range(2)]
                wub = [e4a(nc.sbuf_tensor(f"wub{i}", [128, 8 * 1024], BF16))
                       for i in range(2)]
                gsil = e4a(nc.sbuf_tensor("gsil", [128, 1024], F32))
                gu_t = e4a(nc.sbuf_tensor("gu_t", [128, 1024], F32))

                ld_wb = {}
                pe_wb_use = {}
                for ib in range(4):
                    @on("sync")
                    def _(s, ib=ib):
                        s.wait_ge(CC.h, CC.n)
                        if ib >= 2:
                            s.wait_ge(PE.h, pe_wb_use.get(ib - 2, 0))
                        for k in range(8):
                            LD.inc(s.dma_start(
                                out=wgb[ib % 2][:, 1024 * k:1024 * (k + 1)],
                                in_=cout_wg[k, :, 1024 * ib:1024 * (ib + 1)]), 16)
                            LD.inc(s.dma_start(
                                out=wub[ib % 2][:, 1024 * k:1024 * (k + 1)],
                                in_=cout_wu[k, :, 1024 * ib:1024 * (ib + 1)]), 16)
                    ld_wb[ib] = LD.n

                    g_guev = 0
                    for tt in range(16):
                        @on("tensor")
                        def _(p, ib=ib, tt=tt, g=g_guev):
                            p.wait_ge(LD.h, ld_wb[ib])
                            p.wait_ge(SE.h, n_se_xqT)
                            if g:
                                p.wait_ge(VE.h, g)
                            for half in range(2):
                                for k in range(8):
                                    lhs = xqT[:, D * tt + 128 * k:D * tt + 128 * (k + 1)]
                                    li = p.matmul(ps0[:, 512 * half:512 * (half + 1)], lhs,
                                             wgb[ib % 2][:, 1024 * k + 512 * half:
                                                          1024 * k + 512 * (half + 1)],
                                             start=(k == 0), stop=(k == 7))
                            for half in range(2):
                                for k in range(8):
                                    lhs = xqT[:, D * tt + 128 * k:D * tt + 128 * (k + 1)]
                                    li = p.matmul(ps1[:, 512 * half:512 * (half + 1)], lhs,
                                             wub[ib % 2][:, 1024 * k + 512 * half:
                                                          1024 * k + 512 * (half + 1)],
                                             start=(k == 0), stop=(k == 7))
                            PE.inc(li)
                        n_pe_gu = PE.n
                        pe_wb_use[ib] = PE.n

                        @on("scalar")
                        def _(sc, ib=ib, tt=tt):
                            sc.wait_ge(PE.h, n_pe_gu)
                            if True:
                                sc.wait_ge(VE.h, VE.n)
                            SE.inc(sc.activation(gsil[:], ps0[:], AF.Silu,
                                                 bias=0.0,
                                                 scale=sxf[:, 16 + tt:17 + tt]))
                        n_se_gs = SE.n

                        @on("vector")
                        def _(v, ib=ib, tt=tt):
                            v.wait_ge(SE.h, n_se_gs)
                            if True:
                                v.wait_ge(ST.h, ST.n)
                            li = v.scalar_tensor_tensor(gu_t[:], ps1[:],
                                                   sxf[:, 32 + tt:33 + tt], gsil[:],
                                                   ALU.mult, ALU.mult)
                            li = v.tensor_reduce(scr[:, 0:1], gu_t[:], mybir.AxisListType.X,
                                            ALU.max, apply_absolute_value=True)
                            li = v.tensor_max(vspB[:], vspA[:], vspA[:])
                            if ib == 0:
                                li = v.tensor_copy(gumax[:, tt:tt + 1], scr[:, 0:1])
                            else:
                                li = v.tensor_max(gumax[:, tt:tt + 1], gumax[:, tt:tt + 1],
                                             scr[:, 0:1])
                            VE.inc(li)
                        n_ve_gu = VE.n
                        g_guev = VE.n

                        @on("sync")
                        def _(s, ib=ib, tt=tt):
                            s.wait_ge(VE.h, n_ve_gu)
                            ST.inc(s.dma_start(
                                out=gu_park[128 * tt:128 * (tt + 1),
                                            1024 * ib:1024 * (ib + 1)],
                                in_=gu_t[:]), 16)
                n_st_gu = ST.n
                if "gu" in dbg_outs:
                    @on("sync")
                    def _(s):
                        s.wait_ge(ST.h, n_st_gu)
                        ST.inc(s.dma_start(out=dbg_outs["gu"], in_=gu_park), 16)
                b4a = barrier_vals()

            with ExitStack() as st4b:
                e4b = st4b.enter_context
                wdq_sb = e4b(nc.sbuf_tensor("wdq_sb", [128, 32 * 1024], BF16))
                guqT = e4b(nc.sbuf_tensor("guqT", [128, 4096], BF16))
                gu4 = [e4b(nc.sbuf_tensor(f"gu4_{i}", [128, 1024], F32))
                       for i in range(2)]
                ffo = e4b(nc.sbuf_tensor("ffo", [128, D], F32))
                outb = e4b(nc.sbuf_tensor("outb", [128, D], F32))

                fst4b = {k: True for k in ("sync", "vector", "scalar", "tensor")}

                def W4b(eng, name):
                    if fst4b[name]:
                        wait_all(eng, b4a)
                        fst4b[name] = False

                @on("sync")
                def _(s):
                    W4b(s, "sync")
                    for kk in range(32):
                        LD.inc(s.dma_start(
                            out=wdq_sb[:, 1024 * kk:1024 * (kk + 1)],
                            in_=cout_wd[kk // 4, 128 * (kk % 4):128 * (kk % 4) + 128, :]),
                            16)
                n_ld_wd = LD.n

                @on("vector")
                def _(v):
                    W4b(v, "vector")
                    li = v.tensor_single_scalar(gumax[:], gumax[:], 1e-5, ALU.max)
                    v.tensor_max(vspB[:, 0:256], vspA[:, 0:256], vspA[:, 0:256])
                    li = v.reciprocal(scr[:, 0:16], gumax[:])
                    v.tensor_max(vspB[:, 0:256], vspA[:, 0:256], vspA[:, 0:256])
                    li = v.tensor_scalar_mul(sxf[:, 0:16], scr[:, 0:16], 127.0)
                    li = v.tensor_scalar_mul(scr[:, 0:16], gumax[:], 1.0 / 127.0)
                    v.tensor_max(vspB[:, 0:256], vspA[:, 0:256], vspA[:, 0:256])
                    for tt in range(16):
                        li = v.tensor_mul(sxf[:, 32 + tt:33 + tt], scr[:, tt:tt + 1],
                                     sxf[:, 63:64])
                    VE.inc(li)
                n_ve_s2 = VE.n

                g_dtr = 0
                g_gu4 = [0, 0]
                g_vq4 = 0
                g_pt4 = 0
                g_vout = 0
                for tt in range(16):
                    for ib in range(4):
                        @on("sync")
                        def _(s, tt=tt, ib=ib, g=g_gu4[ib % 2]):
                            W4b(s, "sync")
                            s.wait_ge(ST.h, n_st_gu)
                            if g:
                                s.wait_ge(SE.h, g)
                            LD.inc(s.dma_start(
                                out=gu4[ib % 2][:],
                                in_=gu_park[128 * tt:128 * (tt + 1),
                                            1024 * ib:1024 * (ib + 1)]), 16)
                        n_ld_gu4 = LD.n

                        @on("scalar")
                        def _(sc, tt=tt, ib=ib, gq=g_vq4):
                            W4b(sc, "scalar")
                            sc.wait_ge(LD.h, n_ld_gu4)
                            sc.wait_ge(VE.h, n_ve_s2)
                            if gq:
                                sc.wait_ge(VE.h, gq)
                            SE.inc(sc.activation(sdump[:, 0:1024], gu4[ib % 2][:],
                                                 AF.Copy, bias=0.0,
                                                 scale=sxf[:, tt:tt + 1]))
                        n_se_q4 = SE.n
                        g_gu4[ib % 2] = SE.n

                        @on("vector")
                        def _(v, tt=tt, ib=ib, gp=g_pt4):
                            v.wait_ge(SE.h, n_se_q4)
                            if gp:
                                v.wait_ge(PE.h, gp)
                            li = v.tensor_copy(q_i32[:, 0:1024], sdump[:, 0:1024])
                            VE.inc(v.tensor_copy(q_bf[:, 0:1024], q_i32[:, 0:1024]))
                        n_ve_q4 = VE.n
                        g_vq4 = VE.n

                        @on("tensor")
                        def _(p, tt=tt, ib=ib, g=g_dtr):
                            W4b(p, "tensor")
                            p.wait_ge(VE.h, n_ve_q4)
                            if g:
                                p.wait_ge(SE.h, g)
                            for j in range(8):
                                li = p.transpose(ps3[:, 128 * j:128 * (j + 1)],
                                            q_bf[:, 128 * j:128 * (j + 1)], eye_fs[:])
                            PE.inc(li)
                        n_pe_t4 = PE.n
                        g_pt4 = PE.n

                        @on("scalar")
                        def _(sc, tt=tt, ib=ib):
                            sc.wait_ge(PE.h, n_pe_t4)
                            for j in range(8):
                                kk = 8 * ib + j
                                li = sc.activation(guqT[:, 128 * kk:128 * (kk + 1)],
                                              ps3[:, 128 * j:128 * (j + 1)],
                                              AF.Copy, bias=0.0, scale=1.0)
                            SE.inc(li)
                        g_dtr = SE.n
                    n_se_guqT = SE.n

                    @on("tensor")
                    def _(p, tt=tt):
                        p.wait_ge(SE.h, n_se_guqT)
                        p.wait_ge(LD.h, n_ld_wd)
                        if tt >= 1:
                            p.wait_ge(SE.h, se_ffo)
                        for half in range(2):
                            for kk in range(32):
                                li = p.matmul(ps2[:, 512 * half:512 * (half + 1)],
                                         guqT[:, 128 * kk:128 * (kk + 1)],
                                         wdq_sb[:, 1024 * kk + 512 * half:
                                                 1024 * kk + 512 * (half + 1)],
                                         start=(kk == 0), stop=(kk == 31))
                        PE.inc(li)
                    n_pe_dn = PE.n

                    @on("scalar")
                    def _(sc, tt=tt, gv=g_vout):
                        sc.wait_ge(PE.h, n_pe_dn)
                        if gv:
                            sc.wait_ge(VE.h, gv)
                        li = sc.activation(ffo[:], ps2[:], AF.Copy, bias=0.0,
                                      scale=sxf[:, 32 + tt:33 + tt])
                        li = sc.activation(sdump[:, 0:D], ffo[:], AF.Copy, bias=0.0,
                                      scale=1.0, accum_out=scr[:, 16:17])
                        SE.inc(sc.activation(sdump[:, 0:D], ffo[:], AF.Square,
                                             bias=0.0, scale=1.0,
                                             accum_out=scr[:, 17:18]))
                    se_ffo = SE.n
                    n_se_f = SE.n

                    @on("vector")
                    def _(v):
                        v.wait_ge(SE.h, n_se_f)
                        li = v.tensor_scalar_mul(scr[:, 18:19], scr[:, 16:17], -1.0 / D)
                        li = v.tensor_scalar_mul(scr[:, 19:20], scr[:, 17:18], 1.0 / D)
                        li = v.tensor_mul(scr[:, 20:21], scr[:, 18:19], scr[:, 18:19])
                        v.tensor_max(vspB[:, 0:256], vspA[:, 0:256], vspA[:, 0:256])
                        VE.inc(v.tensor_sub(scr[:, 21:22], scr[:, 19:20],
                                            scr[:, 20:21]))
                    n_ve_v4 = VE.n

                    @on("scalar")
                    def _(sc):
                        sc.wait_ge(VE.h, n_ve_v4)
                        SE.inc(sc.activation(scr[:, 22:23], scr[:, 21:22], AF.Sqrt,
                                             bias=eps1_t[:], scale=1.0))
                    n_se_sq4 = SE.n

                    @on("sync")
                    def _(s, tt=tt, gv=(g_vout if tt >= 2 else 0)):
                        if gv:
                            s.wait_ge(VE.h, gv)
                        LD.inc(s.dma_start(out=x4_t[tt % 2][:],
                                           in_=x1_park[128 * tt:128 * (tt + 1), :]), 16)
                    n_ld_x1b = LD.n

                    @on("vector")
                    def _(v, tt=tt):
                        v.wait_ge(SE.h, n_se_sq4)
                        v.wait_ge(LD.h, n_ld_x1b)
                        if tt >= 1:
                            v.wait_ge(ST.h, st_out)
                        li = v.reciprocal(scr[:, 23:24], scr[:, 22:23])
                        v.tensor_max(vspB[:, 0:256], vspA[:, 0:256], vspA[:, 0:256])
                        li = v.tensor_mul(scr[:, 24:25], scr[:, 18:19], scr[:, 23:24])
                        v.tensor_max(vspB[:, 0:256], vspA[:, 0:256], vspA[:, 0:256])
                        li = v.tensor_scalar(outb[:], ffo[:], scr[:, 23:24], scr[:, 24:25],
                                        ALU.mult, ALU.add)
                        VE.inc(v.tensor_add(outb[:], outb[:], x4_t[tt % 2][:]))
                    n_ve_out = VE.n
                    g_vout = VE.n

                    @on("sync")
                    def _(s, tt=tt):
                        s.wait_ge(VE.h, n_ve_out)
                        ST.inc(s.dma_start(out=out[128 * tt:128 * (tt + 1), :],
                                           in_=outb[:]), 16)
                        if "ffn" in dbg_outs:
                            ST.inc(s.dma_start(
                                out=dbg_outs["ffn"][128 * tt:128 * (tt + 1), :],
                                in_=ffo[:]), 16)
                    st_out = ST.n

            @on("sync")
            def _(s):
                s.wait_ge(ST.h, ST.n)

    return nc, dbg_outs


def _host_layout(inputs, cid):
    x = np.asarray(inputs["L0_attn_x"])[0]
    in_proj_w = np.ascontiguousarray(inputs["L0_attn_in_proj_w"])
    conv_w = np.asarray(inputs["L0_attn_conv_w"])
    conv_b = np.asarray(inputs["L0_attn_conv_b"])
    x_proj_w = np.asarray(inputs["L0_attn_x_proj_w"])
    dt_proj_w = np.asarray(inputs["L0_attn_dt_proj_w"])
    dt_proj_b = np.asarray(inputs["L0_attn_dt_proj_b"])
    a_log = np.asarray(inputs["L0_attn_A_log"])
    d_par = np.asarray(inputs["L0_attn_D"])
    out_proj_w = np.asarray(inputs["L0_attn_out_proj_w"])
    norm_w = np.asarray(inputs["L0_attn_norm_w"])
    w_gate = np.asarray(inputs["L0_ffn_w_gate"])
    w_up = np.asarray(inputs["L0_ffn_w_up"])
    w_down = np.asarray(inputs["L0_ffn_w_down"])

    t0 = TLOC * cid
    x_ext = np.zeros((TEXT, D), np.float32)
    lo = max(0, t0 - HALO)
    x_ext[HALO - (t0 - lo):HALO + TLOC] = x[lo:t0 + TLOC]

    diag = np.zeros((16, 4, 128, 128), np.float32)
    for m in range(16):
        for k in range(4):
            np.fill_diagonal(diag[m, k], conv_w[128 * m:128 * (m + 1), 0, k])

    xprojT = np.zeros((128, 16 * 96), np.float32)
    xpT = np.ascontiguousarray(x_proj_w.T)
    for m in range(16):
        xprojT[:, 96 * m:96 * (m + 1)] = xpT[128 * m:128 * (m + 1), :]

    sel_h = np.zeros((DS, DS * 128), np.float32)
    for s_i in range(DS):
        sel_h[s_i, 128 * s_i:128 * (s_i + 1)] = 1.0

    ch0 = ILOC * cid
    im = {
        "x_ext": x_ext,
        "cid_in": np.array([[cid]], np.int32),
        "w1T": np.ascontiguousarray(in_proj_w.T),
        "conv_diag": diag,
        "conv_b": np.ascontiguousarray(conv_b.reshape(16, 128).T),
        "xprojT": xprojT,
        "dtwT": np.ascontiguousarray(dt_proj_w.T[:, ch0:ch0 + ILOC]),
        "dtb": np.ascontiguousarray(dt_proj_b[ch0:ch0 + ILOC].reshape(ILOC, 1)),
        "a_log": np.ascontiguousarray(a_log[ch0:ch0 + ILOC]),
        "d_par": np.ascontiguousarray(d_par[ch0:ch0 + ILOC].reshape(ILOC, 1)),
        "woT": np.ascontiguousarray(out_proj_w.T),
        "norm_w": np.ascontiguousarray(norm_w.reshape(8, 128).T),
        "eye_f": np.eye(128, dtype=np.float32),
        "eye_b": np.eye(128, dtype=np.float32),
        "eye_r": np.eye(128, dtype=np.float32),
        "sel": sel_h,
        "wgT_sh": np.ascontiguousarray(w_gate.T[128 * cid:128 * (cid + 1), :]),
        "wuT_sh": np.ascontiguousarray(w_up.T[128 * cid:128 * (cid + 1), :]),
        "wdT_sh": np.ascontiguousarray(w_down.T[512 * cid:512 * (cid + 1), :]),
    }
    import ml_dtypes
    im["eye_b"] = im["eye_b"].astype(ml_dtypes.bfloat16)
    return im


_CACHE = {}


def kernel(**inputs):
    key = (tuple(sorted(DEBUG)), MAXPHASE, P1CUT, tuple(sorted(SKIP)))
    if key not in _CACHE:
        _CACHE[key] = build(DEBUG, MAXPHASE, P1CUT, SKIP)
    nc, dbg_outs = _CACHE[key]
    in_maps = [_host_layout(inputs, c) for c in range(NCORE)]
    res = run_bass_kernel_spmd(nc, in_maps, core_ids=list(range(NCORE)))
    outp = np.concatenate([res.results[c]["out"] for c in range(NCORE)], axis=0)
    outp = outp.reshape(1, T, D)
    if DEBUG:
        dbgd = {name: [res.results[c]["dbg_" + name] for c in range(NCORE)]
                for name in dbg_outs}
        return outp, dbgd
    return outp


# version 02
# speedup vs baseline: 1.2592x; 1.2592x over previous
import numpy as np
from contextlib import ExitStack
import concourse.bass as bass
import concourse.mybir as mybir
from concourse.bass_utils import run_bass_kernel_spmd
from concourse.expressions import RuntimeValue

F32 = mybir.dt.float32
F32R = mybir.dt.float32r
BF16 = mybir.dt.bfloat16
I32 = mybir.dt.int32
AF = mybir.ActivationFunctionType
ALU = mybir.AluOpType

NCORE = 8
T = 16384
TLOC = T // NCORE
HALO = 128
TEXT = TLOC + HALO
D = 1024
DI = 2048
DS = 16
DTR = 64
ILOC = DI // NCORE
TC = 1024
NCH = T // TC
EPS1 = 1e-5
EPS2 = 1e-6

DEBUG = []
MAXPHASE = 9
P1CUT = 9
SKIP = set()


class Sem:
    def __init__(self, stack, nc, name):
        self.h = stack.enter_context(nc.semaphore(name))
        self.n = 0

    def inc(self, instr, k=1):
        instr.then_inc(self.h, k)
        self.n += k
        return self.n


def build(debug=(), maxphase=9, p1cut=9, skip=()):
    skip = set(skip)
    nc = bass.Bass("TRN2", target_bir_lowering=False, detect_race_conditions=False)
    dbg = set(debug)

    din = lambda name, shp, dt: nc.dram_tensor(name, shp, dt, kind="ExternalInput").ap()
    x_ext = din("x_ext", [TEXT, D], F32)
    cid_in = din("cid_in", [1, 1], I32)
    w1T = din("w1T", [D, 2 * DI], F32)
    conv_diag = din("conv_diag", [16, 4, 128, 128], F32R)
    conv_b = din("conv_b", [128, 16], F32)
    xprojT = din("xprojT", [128, 16 * 96], F32R)
    dtwT = din("dtwT", [DTR, ILOC], F32R)
    dtb = din("dtb", [ILOC, 1], F32)
    a_log = din("a_log", [ILOC, DS], F32)
    d_par = din("d_par", [ILOC, 1], F32)
    woT = din("woT", [DI, D], F32R)
    norm_w = din("norm_w", [128, 8], F32)
    eye_f = din("eye_f", [128, 128], F32)
    eye_b = din("eye_b", [128, 128], BF16)
    eye_r = din("eye_r", [128, 128], F32R)
    sel = din("sel", [DS, DS * 128], F32R)
    wgT_sh = din("wgT_sh", [128, 4096], F32)
    wuT_sh = din("wuT_sh", [128, 4096], F32)
    wdT_sh = din("wdT_sh", [512, D], F32)

    out = nc.dram_tensor("out", [TLOC, D], F32, kind="ExternalOutput").ap()
    dbg_outs = {}
    if "xn" in dbg:
        dbg_outs["xn"] = nc.dram_tensor("dbg_xn", [TLOC, D], F32,
                                        kind="ExternalOutput").ap()
    if "xn2t" in dbg:
        dbg_outs["xn2t"] = nc.dram_tensor("dbg_xn2t", [1024, TEXT], F32,
                                          kind="ExternalOutput").ap()
    if "xconv" in dbg:
        dbg_outs["xconv"] = nc.dram_tensor("dbg_xconv", [DI, TLOC], F32,
                                           kind="ExternalOutput").ap()
    if "xdbc" in dbg:
        dbg_outs["xdbc"] = nc.dram_tensor("dbg_xdbc", [96, TLOC], F32,
                                          kind="ExternalOutput").ap()
    if "y" in dbg:
        dbg_outs["y"] = nc.dram_tensor("dbg_y", [16 * ILOC, TC], F32,
                                       kind="ExternalOutput").ap()
    if "x1" in dbg:
        dbg_outs["x1"] = nc.dram_tensor("dbg_x1", [TLOC, D], F32,
                                        kind="ExternalOutput").ap()
    if "sxf" in dbg:
        dbg_outs["sxf"] = nc.dram_tensor("dbg_sxf", [128, 64], F32,
                                         kind="ExternalOutput").ap()
    if "xqt" in dbg:
        dbg_outs["xqt"] = nc.dram_tensor("dbg_xqt", [128, T], BF16,
                                         kind="ExternalOutput").ap()
    if "xcm" in dbg:
        dbg_outs["xcm"] = nc.dram_tensor("dbg_xcm", [DI, TLOC], F32,
                                         kind="ExternalOutput").ap()
    if "ffn" in dbg:
        dbg_outs["ffn"] = nc.dram_tensor("dbg_ffn", [TLOC, D], F32,
                                         kind="ExternalOutput").ap()
    if "gu" in dbg:
        dbg_outs["gu"] = nc.dram_tensor("dbg_gu", [TLOC, 4096], F32,
                                        kind="ExternalOutput").ap()

    z_park = nc.dram_tensor("z_park", [DI, TLOC], F32).ap()
    xn_park = nc.dram_tensor("xn_park", [TLOC, D], F32).ap()
    x1_park = nc.dram_tensor("x1_park", [TLOC, D], F32).ap()
    gu_park = nc.dram_tensor("gu_park", [TLOC, 4096], F32).ap()
    g_park = nc.dram_tensor("g_park", [DI, TLOC], F32R).ap()
    cin_xc = nc.dram_tensor("cin_xc", [DI, TLOC], F32R).ap()
    cout_xc = nc.dram_tensor("cout_xc", [NCORE * DI, TLOC], F32R).ap()
    cin_db = nc.dram_tensor("cin_db", [96, TLOC], F32R).ap()
    cout_db = nc.dram_tensor("cout_db", [NCORE, 96, TLOC], F32R).ap()
    cin_y = nc.dram_tensor("cin_y", [16 * ILOC, TC], F32).ap()
    cout_y = nc.dram_tensor("cout_y", [16 * NCORE * ILOC, TC], F32).ap()
    xcm_dram = nc.dram_tensor("xcm_dram", [DI, TLOC], F32R).ap()
    ym_dram = nc.dram_tensor("ym_dram", [DI, TLOC], F32).ap()
    cin_ws = nc.dram_tensor("cin_ws", [1, 4], F32).ap()
    cout_ws = nc.dram_tensor("cout_ws", [1, 4], F32).ap()
    cin_wg = nc.dram_tensor("cin_wg", [128, 4096], BF16).ap()
    cout_wg = nc.dram_tensor("cout_wg", [NCORE, 128, 4096], BF16).ap()
    cin_wu = nc.dram_tensor("cin_wu", [128, 4096], BF16).ap()
    cout_wu = nc.dram_tensor("cout_wu", [NCORE, 128, 4096], BF16).ap()
    cin_wd = nc.dram_tensor("cin_wd", [512, D], BF16).ap()
    cout_wd = nc.dram_tensor("cout_wd", [NCORE, 512, D], BF16).ap()
    rg = [list(range(NCORE))]

    with ExitStack() as st:
        e = st.enter_context

        def sb(name, shp, dt):
            return e(nc.sbuf_tensor(name, shp, dt))

        dtwT_s = sb("dtwT_s", [DTR, ILOC], F32R)
        sel_s = sb("sel_s", [DS, DS * 128], F32R)
        eye_fs = sb("eye_fs", [128, 128], F32)
        eye_bs = sb("eye_bs", [128, 128], BF16)
        eye_rs = sb("eye_rs", [128, 128], F32R)
        aneg = sb("aneg", [128, 2 * DS], F32)
        dpar_s = sb("dpar_s", [128, 2], F32)
        dtbn = sb("dtbn", [128, 2], F32)
        normw_s = sb("normw_s", [128, 8], F32)
        convb_s = sb("convb_s", [128, 16], F32)
        ones_col = sb("ones_col", [128, 1], F32)
        ones_row = sb("ones_row", [1, 128], F32)
        one_t = sb("one_t", [128, 1], F32)
        eps1_t = sb("eps1_t", [128, 1], F32)
        eps2_t = sb("eps2_t", [128, 1], F32)
        hlast = [sb(f"hlast_{it}", [128, DS], F32) for it in range(2)]
        scr = sb("scr", [128, 32], F32)
        vspA = sb("vspA", [128, 512], F32)
        vspB = sb("vspB", [128, 512], F32)
        sdump = sb("sdump", [128, 1024], F32)
        swrep = sb("swrep", [128, 4], F32)
        sxf = sb("sxf", [128, 64], F32)
        scid = sb("scid", [1, 1], I32)

        LD = Sem(st, nc, "LD")
        ST = Sem(st, nc, "ST")
        PE = Sem(st, nc, "PE")
        VE = Sem(st, nc, "VE")
        SE = Sem(st, nc, "SE")
        CC = Sem(st, nc, "CC")
        SEMS = [LD, ST, PE, VE, SE, CC]

        ps0 = e(nc.psum_tensor("ps0", [128, 1024], F32))
        ps1 = e(nc.psum_tensor("ps1", [128, 1024], F32))
        ps2 = e(nc.psum_tensor("ps2", [128, 1024], F32))
        ps3 = e(nc.psum_tensor("ps3", [128, 1024], F32))

        block = e(nc.Block())

        def on(name):
            return getattr(block, name)

        def barrier_vals():
            return [(S, S.n) for S in SEMS]

        def wait_all(eng, bv):
            for S, v in bv:
                if v > 0:
                    eng.wait_ge(S.h, v)

        r_cid = nc.sync.alloc_register("r_cid")

        @on("sync")
        def _(s):
            LD.inc(s.dma_start(out=scid[:], in_=cid_in), 16)
            s.wait_ge(LD.h, LD.n)
            s.load(r_cid, scid[0:1, 0:1])
        cidv = RuntimeValue(r_cid)

        with ExitStack() as stw:
            ew = stw.enter_context

            def sbw(name, shp, dt):
                return ew(nc.sbuf_tensor(name, shp, dt))

            wg_sh = sbw("wg_sh", [128, 4096], F32)
            wu_sh = sbw("wu_sh", [128, 4096], F32)
            wd_sh = [sbw(f"wd_sh{i}", [128, D], F32) for i in range(4)]
            wq_g = sbw("wq_g", [128, 4096], BF16)
            wq_u = sbw("wq_u", [128, 4096], BF16)
            wq_d = [sbw(f"wq_d{i}", [128, D], BF16) for i in range(4)]
            wqi = sbw("wqi", [128, 4096], I32)
            wdump = sbw("wdump", [128, 4096], F32)
            asums = sbw("asums", [128, 8], F32)
            ppart = sbw("ppart", [1, 8], F32)
            arbuf = sbw("arbuf", [1, 4], F32)

            @on("sync")
            def _(s):
                for dst, src in [
                    (wg_sh[:], wgT_sh), (wu_sh[:], wuT_sh),
                    (wd_sh[0][:], wdT_sh[0:128, :]), (wd_sh[1][:], wdT_sh[128:256, :]),
                    (wd_sh[2][:], wdT_sh[256:384, :]), (wd_sh[3][:], wdT_sh[384:512, :]),
                    (sel_s[:], sel), (eye_fs[:], eye_f), (eye_bs[:], eye_b),
                    (eye_rs[:], eye_r), (normw_s[:], norm_w), (convb_s[:], conv_b),
                    (dtwT_s[:], dtwT),
                    (aneg[:, 0:DS], a_log[0:128, :]),
                    (aneg[:, DS:2 * DS], a_log[128:256, :]),
                    (dpar_s[:, 0:1], d_par[0:128, :]),
                    (dpar_s[:, 1:2], d_par[128:256, :]),
                    (dtbn[:, 0:1], dtb[0:128, :]),
                    (dtbn[:, 1:2], dtb[128:256, :]),
                ]:
                    LD.inc(s.dma_start(out=dst, in_=src), 16)
            n_ld_w = LD.n

            @on("vector")
            def _(v):
                li = v.memset(vspA[:], 0.0)
                li = v.memset(ones_col[:], 1.0)
                li = v.memset(ones_row[:], 1.0)
                li = v.memset(one_t[:], 1.0)
                li = v.memset(eps1_t[:], EPS1)
                VE.inc(v.memset(eps2_t[:], EPS2))

            @on("scalar")
            def _(sc):
                sc.wait_ge(LD.h, n_ld_w)
                li = sc.activation(wdump[:], wg_sh[:], AF.Abs, bias=0.0, scale=1.0,
                              accum_out=asums[:, 0:1])
                li = sc.activation(wdump[:], wu_sh[:], AF.Abs, bias=0.0, scale=1.0,
                              accum_out=asums[:, 1:2])
                for i in range(4):
                    li = sc.activation(wdump[:, 0:D], wd_sh[i][:], AF.Abs, bias=0.0,
                                  scale=1.0, accum_out=asums[:, 4 + i:5 + i])
                SE.inc(sc.activation(aneg[:], aneg[:], AF.Exp, bias=0.0, scale=1.0))
            n_se_abs = SE.n

            @on("vector")
            def _(v):
                v.wait_ge(SE.h, n_se_abs)
                li = v.tensor_add(asums[:, 2:3], asums[:, 4:5], asums[:, 5:6])
                li = v.tensor_add(asums[:, 3:4], asums[:, 6:7], asums[:, 7:8])
                v.tensor_max(vspB[:, 0:256], vspA[:, 0:256], vspA[:, 0:256])
                li = v.tensor_add(asums[:, 2:3], asums[:, 2:3], asums[:, 3:4])
                VE.inc(v.tensor_scalar_mul(aneg[:], aneg[:], -1.0))
            n_ve_as = VE.n

            @on("tensor")
            def _(p):
                p.wait_ge(VE.h, n_ve_as)
                PE.inc(p.matmul(ps0[0:1, 0:8], ones_col[:], asums[:, 0:8],
                                start=True, stop=True))
            n_pe_as = PE.n

            @on("scalar")
            def _(sc):
                sc.wait_ge(PE.h, n_pe_as)
                SE.inc(sc.activation(ppart[:], ps0[0:1, 0:8], AF.Copy,
                                     bias=0.0, scale=1.0))
            n_se_pp = SE.n

            @on("sync")
            def _(s):
                s.wait_ge(SE.h, n_se_pp)
                ST.inc(s.dma_start(out=cin_ws, in_=ppart[:, 0:4]), 16)
            n_st_ws = ST.n

            @on("gpsimd")
            def _(g):
                g.wait_ge(ST.h, n_st_ws)
                CC.inc(g.collective_compute("AllReduce", ALU.add, rg,
                                            [cin_ws.opt()], [cout_ws.opt()]))
            n_cc_ar = CC.n

            @on("sync")
            def _(s):
                s.wait_ge(CC.h, n_cc_ar)
                LD.inc(s.dma_start(out=arbuf[:], in_=cout_ws), 16)
            n_ld_ar = LD.n

            @on("vector")
            def _(v):
                v.wait_ge(LD.h, n_ld_ar)
                cnt = 1.0 / (4096.0 * D)
                li = v.tensor_scalar_mul(arbuf[:, 0:3], arbuf[:, 0:3], cnt)
                v.tensor_max(vspB[:, 0:256], vspA[:, 0:256], vspA[:, 0:256])
                li = v.tensor_single_scalar(arbuf[:, 0:3], arbuf[:, 0:3], 1e-5, ALU.max)
                v.tensor_max(vspB[:, 0:256], vspA[:, 0:256], vspA[:, 0:256])
                li = v.tensor_copy(ppart[:, 3:6], arbuf[:, 0:3])
                VE.inc(v.reciprocal(ppart[:, 0:3], arbuf[:, 0:3]))
            n_ve_sw = VE.n

            @on("tensor")
            def _(p):
                p.wait_ge(VE.h, n_ve_sw)
                PE.inc(p.matmul(ps0[:, 8:16], ones_row[0:1, :], ppart[:, 0:8],
                                start=True, stop=True))
            n_pe_sw = PE.n

            @on("scalar")
            def _(sc):
                sc.wait_ge(PE.h, n_pe_sw)
                li = sc.activation(swrep[:, 0:3], ps0[:, 8:11], AF.Copy, bias=0.0, scale=1.0)
                SE.inc(sc.activation(sxf[:, 61:64], ps0[:, 11:14], AF.Copy,
                                     bias=0.0, scale=1.0))
            n_se_sw = SE.n

            @on("scalar")
            def _(sc):
                sc.wait_ge(SE.h, n_se_sw)
                li = sc.activation(wdump[:, 0:256], wdump[:, 256:512], AF.Copy,
                              bias=0.0, scale=1.0)
                li = sc.activation(wg_sh[:], wg_sh[:], AF.Copy, bias=0.0, scale=swrep[:, 0:1])
                li = sc.activation(wu_sh[:], wu_sh[:], AF.Copy, bias=0.0, scale=swrep[:, 1:2])
                for i in range(4):
                    li = sc.activation(wd_sh[i][:], wd_sh[i][:], AF.Copy, bias=0.0,
                                  scale=swrep[:, 2:3])
                SE.inc(li)
            n_se_ws = SE.n

            @on("vector")
            def _(v):
                v.wait_ge(SE.h, n_se_ws)
                li = v.tensor_copy(wqi[:], wg_sh[:])
                li = v.tensor_scalar(wq_g[:], wqi[:], -1.0, 1.0, ALU.max, ALU.min)
                li = v.tensor_copy(wqi[:], wu_sh[:])
                li = v.tensor_scalar(wq_u[:], wqi[:], -1.0, 1.0, ALU.max, ALU.min)
                for i in range(4):
                    li = v.tensor_copy(wqi[:, 0:D], wd_sh[i][:])
                    li = v.tensor_scalar(wq_d[i][:], wqi[:, 0:D], -1.0, 1.0,
                                    ALU.max, ALU.min)
                VE.inc(li)
            n_ve_wq = VE.n

            @on("sync")
            def _(s):
                s.wait_ge(VE.h, n_ve_wq)
                ST.inc(s.dma_start(out=cin_wg, in_=wq_g[:]), 16)
                ST.inc(s.dma_start(out=cin_wu, in_=wq_u[:]), 16)
                for i in range(4):
                    ST.inc(s.dma_start(out=cin_wd[128 * i:128 * (i + 1), :],
                                       in_=wq_d[i][:]), 16)
            n_st_wq = ST.n

            @on("gpsimd")
            def _(g):
                g.wait_ge(ST.h, n_st_wq)
                CC.inc(g.collective_compute("AllGather", ALU.bypass, rg,
                                            [cin_wg.opt()], [cout_wg.opt()]))
                CC.inc(g.collective_compute("AllGather", ALU.bypass, rg,
                                            [cin_wu.opt()], [cout_wu.opt()]))
                CC.inc(g.collective_compute("AllGather", ALU.bypass, rg,
                                            [cin_wd.opt()], [cout_wd.opt()]))
            bw = barrier_vals()

        if maxphase < 1:
            @on("sync")
            def _(s):
                wait_all(s, bw)
                s.wait_ge(ST.h, ST.n)
            return nc, dbg_outs

        with ExitStack() as st1:
            e1 = st1.enter_context

            def sb1(name, shp, dt):
                return e1(nc.sbuf_tensor(name, shp, dt))

            xn2T = [sb1(f"xn2T_{j}", [128, TEXT], F32R) for j in range(8)]
            xprojT_s = sb1("xprojT_s", [128, 16 * 96], F32R)
            xtile = [sb1(f"xtile{i}", [128, D], F32) for i in range(2)]
            xn_t = sb1("xn_t", [128, D], F32)
            xn2_t = sb1("xn2_t", [128, D], F32)
            w1f = [sb1(f"w1f{k}", [128, 256], F32) for k in range(8)]
            w1r = [[sb1(f"w1r{pp}_{k}", [128, 256], F32R) for k in range(8)]
                   for pp in range(2)]
            xs_t = [sb1(f"xs_t{i}", [128, TEXT], F32R) for i in range(2)]
            zevac = sb1("zevac", [128, TLOC], F32)
            diag_t = [sb1(f"diag_t{i}", [128, 512], F32R) for i in range(2)]
            xcv_t = [sb1(f"xcv_t{i}", [128, TLOC], F32R) for i in range(2)]
            xdbc_s = sb1("xdbc_s", [96, TLOC], F32R)

            fst = {k: True for k in ("sync", "vector", "scalar", "tensor")}

            def W1(eng, name):
                if fst[name]:
                    wait_all(eng, bw)
                    fst[name] = False

            @on("sync")
            def _(s):
                W1(s, "sync")
                LD.inc(s.dma_start(out=xprojT_s[:], in_=xprojT), 16)

            NT = TEXT // 128
            g_xtile = [0, 0]
            g_xnst = 0
            g_x2tr = 0
            g_scr = 0
            se_trv = 0
            for tt in range(NT):
                buf = xtile[tt % 2]

                @on("sync")
                def _(s, tt=tt, buf=buf, g=g_xtile[tt % 2]):
                    if g:
                        s.wait_ge(VE.h, g)
                    LD.inc(s.dma_start(out=buf[:],
                                       in_=x_ext[128 * tt:128 * (tt + 1), :]), 16)
                n_ld_x = LD.n

                @on("scalar")
                def _(sc, buf=buf, gs=g_scr):
                    W1(sc, "scalar")
                    sc.wait_ge(LD.h, n_ld_x)
                    if gs:
                        sc.wait_ge(VE.h, gs)
                    li = sc.activation(sdump[:, 0:D], buf[:], AF.Copy, bias=0.0, scale=1.0,
                                  accum_out=scr[:, 0:1])
                    SE.inc(sc.activation(sdump[:, 0:D], buf[:], AF.Square, bias=0.0,
                                         scale=1.0, accum_out=scr[:, 1:2]))
                n_se_st = SE.n

                @on("vector")
                def _(v):
                    W1(v, "vector")
                    v.wait_ge(SE.h, n_se_st)
                    li = v.tensor_scalar_mul(scr[:, 2:3], scr[:, 0:1], -1.0 / D)
                    li = v.tensor_scalar_mul(scr[:, 3:4], scr[:, 1:2], 1.0 / D)
                    li = v.tensor_mul(scr[:, 4:5], scr[:, 2:3], scr[:, 2:3])
                    v.tensor_max(vspB[:, 0:256], vspA[:, 0:256], vspA[:, 0:256])
                    VE.inc(v.tensor_sub(scr[:, 5:6], scr[:, 3:4], scr[:, 4:5]))
                n_ve_v = VE.n

                @on("scalar")
                def _(sc):
                    sc.wait_ge(VE.h, n_ve_v)
                    SE.inc(sc.activation(scr[:, 6:7], scr[:, 5:6], AF.Sqrt,
                                         bias=eps1_t[:], scale=1.0))
                n_se_s1 = SE.n

                @on("vector")
                def _(v):
                    v.wait_ge(SE.h, n_se_s1)
                    li = v.reciprocal(scr[:, 7:8], scr[:, 6:7])
                    v.tensor_max(vspB[:, 0:256], vspA[:, 0:256], vspA[:, 0:256])
                    li = v.tensor_mul(scr[:, 8:9], scr[:, 7:8], scr[:, 7:8])
                    v.tensor_max(vspB[:, 0:256], vspA[:, 0:256], vspA[:, 0:256])
                    VE.inc(v.tensor_mul(scr[:, 9:10], scr[:, 8:9], scr[:, 5:6]))
                n_ve_v2 = VE.n

                @on("scalar")
                def _(sc):
                    sc.wait_ge(VE.h, n_ve_v2)
                    SE.inc(sc.activation(scr[:, 10:11], scr[:, 9:10], AF.Sqrt,
                                         bias=eps2_t[:], scale=1.0))
                n_se_s2 = SE.n

                @on("vector")
                def _(v, buf=buf, tt=tt, gxn=g_xnst, gtr=g_x2tr):
                    v.wait_ge(SE.h, n_se_s2)
                    li = v.reciprocal(scr[:, 11:12], scr[:, 10:11])
                    v.tensor_max(vspB[:, 0:256], vspA[:, 0:256], vspA[:, 0:256])
                    li = v.tensor_mul(scr[:, 12:13], scr[:, 7:8], scr[:, 11:12])
                    v.tensor_max(vspB[:, 0:256], vspA[:, 0:256], vspA[:, 0:256])
                    if gxn:
                        v.wait_ge(ST.h, gxn)
                    li = v.tensor_scalar(xn_t[:], buf[:], scr[:, 2:3], scr[:, 7:8],
                                    ALU.add, ALU.mult)
                    if gtr:
                        v.wait_ge(PE.h, gtr)
                    VE.inc(v.tensor_scalar(xn2_t[:], buf[:], scr[:, 2:3],
                                           scr[:, 12:13], ALU.add, ALU.mult))
                n_ve_xn = VE.n
                g_xtile[tt % 2] = n_ve_xn
                g_scr = n_ve_xn

                @on("sync")
                def _(s, tt=tt):
                    s.wait_ge(VE.h, n_ve_xn)
                    if tt >= 1:
                        ST.inc(s.dma_start(out=xn_park[128 * (tt - 1):128 * tt, :],
                                           in_=xn_t[:]), 16)
                g_xnst = ST.n

                @on("tensor")
                def _(p, tt=tt, gse=se_trv):
                    W1(p, "tensor")
                    p.wait_ge(VE.h, n_ve_xn)
                    if gse:
                        p.wait_ge(SE.h, gse)
                    for j in range(8):
                        li = p.transpose(ps0[:, 128 * j:128 * (j + 1)],
                                    xn2_t[:, 128 * j:128 * (j + 1)], eye_fs[:])
                    PE.inc(li)
                n_pe_tr = PE.n
                g_x2tr = n_pe_tr

                @on("scalar")
                def _(sc, tt=tt):
                    sc.wait_ge(PE.h, n_pe_tr)
                    for j in range(8):
                        li = sc.activation(xn2T[j][:, 128 * tt:128 * (tt + 1)],
                                      ps0[:, 128 * j:128 * (j + 1)],
                                      AF.Copy, bias=0.0, scale=1.0)
                    SE.inc(li)
                se_trv = SE.n

            if "xn" in dbg_outs or "xn2t" in dbg_outs:
                @on("sync")
                def _(s):
                    s.wait_ge(SE.h, se_trv)
                    s.wait_ge(ST.h, ST.n)
                    if "xn" in dbg_outs:
                        ST.inc(s.dma_start(out=dbg_outs["xn"], in_=xn_park), 16)
                    if "xn2t" in dbg_outs:
                        for j in range(8):
                            ST.inc(s.dma_start(
                                out=dbg_outs["xn2t"][128 * j:128 * (j + 1), :],
                                in_=xn2T[j][:].bitcast(F32)), 16)

            if p1cut < 1:
                @on("sync")
                def _(s):
                    for S in SEMS:
                        if S.n:
                            s.wait_ge(S.h, S.n)
                return nc, dbg_outs

            NB_IP = [(0, 512), (512, 512), (1024, 512), (1536, 384), (1920, 256)]
            g_ip = [se_trv, se_trv]
            g_cv = [0, 0]
            g_xs = [0, 0]
            g_xcv = [0, 0]
            g_xcvst = [0, 0]
            g_diag = [0, 0]
            g_z = 0
            ld_diag = [0, 0]
            ve_w1 = {}
            ld_w1 = {}
            pe_w1use = {}

            def w1_load(mb):
                pp = mb % 2

                @on("sync")
                def _(s, mb=mb):
                    if mb >= 1:
                        s.wait_ge(VE.h, ve_w1[mb - 1])
                    for k in range(8):
                        LD.inc(s.dma_start(out=w1f[k][:],
                                           in_=w1T[128 * k:128 * (k + 1),
                                                  256 * mb:256 * (mb + 1)]), 16)
                ld_w1[mb] = LD.n

                @on("vector")
                def _(v, mb=mb, pp=pp):
                    v.wait_ge(LD.h, ld_w1[mb])
                    if mb >= 2:
                        v.wait_ge(PE.h, pe_w1use.get(mb - 2, 0))
                    for k in range(8):
                        li = v.tensor_scalar_mul(w1r[pp][k][:], w1f[k][:],
                                            normw_s[:, k:k + 1])
                    VE.inc(li)
                ve_w1[mb] = VE.n

            w1_load(0)
            w1_load(1)

            n_pe_xdbc = 0
            m_count = 32 if p1cut >= 9 else min(p1cut, 32)
            for m in range(m_count):
                mb = m // 2
                pp = mb % 2
                mm = m % 2
                is_xs = m < 16
                if mm == 0 and 1 <= mb <= 14:
                    w1_load(mb + 1)

                if is_xs and "diag" not in skip:
                    @on("sync")
                    def _(s, m=m, g=g_diag[m % 2]):
                        if g:
                            s.wait_ge(PE.h, g)
                        LD.inc(s.dma_start(
                            out=diag_t[m % 2][:].rearrange("p (k q) -> p k q", q=128),
                            in_=conv_diag[m, :, :, :].transpose([1, 0, 2])),
                            16)
                    ld_diag[m % 2] = LD.n

                if "ip" in skip:
                    continue
                for nbi, (col0, nbw) in enumerate(NB_IP):
                    psh = ps0[:, 0:512] if nbi % 2 == 0 else ps0[:, 512:1024]

                    @on("tensor")
                    def _(p, mb=mb, pp=pp, mm=mm, col0=col0, nbw=nbw, psh=psh,
                          g=g_ip[nbi % 2]):
                        p.wait_ge(VE.h, ve_w1[mb])
                        if g:
                            p.wait_ge(SE.h, g)
                        for k in range(8):
                            li = p.matmul(psh[:, 0:nbw],
                                     w1r[pp][k][:, 128 * mm:128 * (mm + 1)],
                                     xn2T[k][:, col0:col0 + nbw],
                                     start=(k == 0), stop=(k == 7))
                        PE.inc(li)
                    n_pe_ip = PE.n
                    if nbi == len(NB_IP) - 1:
                        pe_w1use[mb] = PE.n

                    @on("scalar")
                    def _(sc, m=m, nbi=nbi, col0=col0, nbw=nbw, psh=psh,
                          is_xs=is_xs, gz=g_z, gxs=g_xs[m % 2]):
                        sc.wait_ge(PE.h, n_pe_ip)
                        if is_xs:
                            if nbi == 0 and gxs:
                                sc.wait_ge(PE.h, gxs)
                            li = sc.activation(xs_t[m % 2][:, col0:col0 + nbw],
                                          psh[:, 0:nbw], AF.Copy, bias=0.0, scale=1.0)
                        else:
                            if nbi == 0 and gz:
                                sc.wait_ge(ST.h, gz)
                            if col0 + nbw > HALO:
                                lo = max(col0, HALO)
                                li = sc.activation(zevac[:, lo - HALO:col0 + nbw - HALO],
                                              psh[:, lo - col0:nbw],
                                              AF.Copy, bias=0.0, scale=1.0)
                        SE.inc(li)
                    g_ip[nbi % 2] = SE.n
                n_se_ip = SE.n

                if not is_xs:
                    @on("sync")
                    def _(s, m=m):
                        s.wait_ge(SE.h, n_se_ip)
                        ST.inc(s.dma_start(
                            out=z_park[128 * (m - 16):128 * (m - 15), :],
                            in_=zevac[:]), 16)
                    g_z = ST.n
                    continue
                if "conv" in skip:
                    continue

                for nb2 in range(4):
                    psh = ps1[:, 0:512] if nb2 % 2 == 0 else ps1[:, 512:1024]

                    @on("tensor")
                    def _(p, m=m, nb2=nb2, psh=psh, g=g_cv[nb2 % 2]):
                        p.wait_ge(SE.h, n_se_ip)
                        p.wait_ge(LD.h, ld_diag[m % 2])
                        if g:
                            p.wait_ge(SE.h, g)
                        for k in range(4):
                            base = HALO + 512 * nb2 - 3 + k
                            li = p.matmul(psh[:], diag_t[m % 2][:, 128 * k:128 * (k + 1)],
                                     xs_t[m % 2][:, base:base + 512],
                                     start=(k == 0), stop=(k == 3))
                        PE.inc(li)
                    n_pe_cv = PE.n
                    if nb2 == 3:
                        g_xs[m % 2] = PE.n
                        g_diag[m % 2] = PE.n

                    @on("scalar")
                    def _(sc, m=m, nb2=nb2, psh=psh, gxc=g_xcv[m % 2],
                          gst=g_xcvst[m % 2]):
                        sc.wait_ge(PE.h, n_pe_cv)
                        if nb2 == 0 and gxc:
                            sc.wait_ge(PE.h, gxc)
                        if nb2 == 0 and gst:
                            sc.wait_ge(ST.h, gst)
                        li = sc.activation(xcv_t[m % 2][:, 512 * nb2:512 * (nb2 + 1)],
                                      psh[:], AF.Silu, bias=convb_s[:, m:m + 1],
                                      scale=1.0)
                        SE.inc(li)
                    g_cv[nb2 % 2] = SE.n
                n_se_cv = SE.n

                @on("sync")
                def _(s, m=m):
                    s.wait_ge(SE.h, n_se_cv)
                    ST.inc(s.dma_start(out=cin_xc[128 * m:128 * (m + 1), :],
                                       in_=xcv_t[m % 2][:]), 16)
                g_xcvst[m % 2] = ST.n

                if m % 2 == 1 and XC_INLINE:
                    @on("gpsimd")
                    def _(g, k=m // 2, st=ST.n):
                        g.wait_ge(ST.h, st)
                        CC.inc(g.collective_compute(
                            "AllGather", ALU.bypass, rg,
                            [cin_xc[ILOC * k:ILOC * (k + 1), :].opt()],
                            [cout_xc[NCORE * ILOC * k:
                                     NCORE * ILOC * (k + 1), :].opt()]))

                if "xdbc" not in skip:
                    @on("tensor")
                    def _(p, m=m):
                        p.wait_ge(SE.h, n_se_cv)
                        for nb2 in range(4):
                            tgt = ps2 if nb2 < 2 else ps3
                            pw = tgt[0:96, 512 * (nb2 % 2):512 * (nb2 % 2) + 512]
                            li = p.matmul(pw, xprojT_s[:, 96 * m:96 * (m + 1)],
                                     xcv_t[m % 2][:, 512 * nb2:512 * (nb2 + 1)],
                                     start=(m == 0), stop=(m == 15))
                        PE.inc(li)
                    g_xcv[m % 2] = PE.n
                    n_pe_xdbc = PE.n

            if p1cut < 33 and p1cut != 9:
                @on("sync")
                def _(s):
                    for S in SEMS:
                        if S.n:
                            s.wait_ge(S.h, S.n)
                return nc, dbg_outs

            @on("scalar")
            def _(sc):
                sc.wait_ge(PE.h, n_pe_xdbc)
                for nb2 in range(4):
                    tgt = ps2 if nb2 < 2 else ps3
                    li = sc.activation(xdbc_s[:, 512 * nb2:512 * nb2 + 512],
                                  tgt[0:96, 512 * (nb2 % 2):512 * (nb2 % 2) + 512],
                                  AF.Copy, bias=0.0, scale=1.0)
                SE.inc(li)
            n_se_db = SE.n

            @on("sync")
            def _(s):
                s.wait_ge(SE.h, n_se_db)
                ST.inc(s.dma_start(out=cin_db, in_=xdbc_s[:]), 16)
            n_st_db = ST.n

            @on("gpsimd")
            def _(g):
                g.wait_ge(ST.h, n_st_db)
                if not XC_INLINE:
                    for k in range(NCORE):
                        CC.inc(g.collective_compute(
                            "AllGather", ALU.bypass, rg,
                            [cin_xc[ILOC * k:ILOC * (k + 1), :].opt()],
                            [cout_xc[NCORE * ILOC * k:
                                     NCORE * ILOC * (k + 1), :].opt()]))
                CC.inc(g.collective_compute("AllGather", ALU.bypass, rg,
                                            [cin_db.opt()], [cout_db.opt()]))
            n_cc_p1 = CC.n

            if "xconv" in dbg_outs:
                @on("sync")
                def _(s):
                    s.wait_ge(CC.h, n_cc_p1)
                    ST.inc(s.dma_start(out=dbg_outs["xconv"],
                                       in_=cin_xc.bitcast(F32)), 16)
            if "xdbc" in dbg_outs:
                @on("sync")
                def _(s):
                    s.wait_ge(CC.h, n_cc_p1)
                    ST.inc(s.dma_start(out=dbg_outs["xdbc"],
                                       in_=cin_db.bitcast(F32)), 16)
            b1 = barrier_vals()

        if maxphase < 2:
            @on("sync")
            def _(s):
                wait_all(s, b1)
                s.wait_ge(ST.h, ST.n)
            return nc, dbg_outs

        with ExitStack() as st2:
            e2 = st2.enter_context

            def sb2(name, shp, dt):
                return e2(nc.sbuf_tensor(name, shp, dt))

            dtw_w = [sb2(f"dtw_w{i}", [DTR, TC], F32R) for i in range(2)]
            b_w = [sb2(f"b_w{i}", [DS, TC], F32R) for i in range(2)]
            c_w = [sb2(f"c_w{i}", [DS, TC], F32R) for i in range(2)]
            xcm = [[sb2(f"xcm{it}_{i}", [128, TC], F32R) for i in range(2)]
                   for it in range(2)]
            spl_a = sb2("spl_a", [128, TC], F32)
            spl_r = sb2("spl_r", [128, TC], F32)
            dcp = [sb2(f"dcp{it}", [128, TC], F32) for it in range(2)]
            du_t = [sb2(f"du{it}", [128, TC], F32) for it in range(2)]
            dab = [[sb2(f"dab{it}_{i}", [128, TC], F32) for i in range(2)]
                   for it in range(2)]
            dub = [[sb2(f"dub{it}_{i}", [128, TC], F32) for i in range(2)]
                   for it in range(2)]
            h_buf = [sb2(f"h_buf{it}", [128, TC], F32) for it in range(2)]
            hcr = [sb2(f"hcr{it}", [128, TC], F32R) for it in range(2)]
            yfull = [sb2(f"yfull{it}", [128, TC], F32) for it in range(2)]

            fst2 = {k: True for k in ("sync", "vector", "scalar", "tensor")}

            def W2(eng, name):
                if fst2[name]:
                    wait_all(eng, b1)
                    fst2[name] = False

            g_xcm = [[0, 0], [0, 0]]
            ld_xcm = [[0, 0], [0, 0]]
            g_dbw = [0, 0]
            ld_dbw = [0, 0]

            @on("sync")
            def _(s):
                W2(s, "sync")
                s.wait_ge(CC.h, n_cc_p1)
                rowb = nc.snap(RuntimeValue(r_cid) * (NCORE * ILOC))
                ST.inc(s.dma_start(
                    out=xcm_dram[:],
                    in_=cout_xc[bass.DynSlice(rowb, NCORE * ILOC), :]), 16)
            n_st_xcslice = ST.n
            if "xcm" in dbg_outs:
                @on("sync")
                def _(s):
                    s.wait_ge(ST.h, n_st_xcslice)
                    ST.inc(s.dma_start(out=dbg_outs["xcm"],
                                       in_=xcm_dram.bitcast(F32)), 16)

            def xcm_load(c):
                blk = c // 2
                col0 = (c % 2) * TC
                for it in range(2):
                    @on("sync")
                    def _(s, it=it, c=c, g=g_xcm[it][c % 2]):
                        W2(s, "sync")
                        s.wait_ge(ST.h, n_st_xcslice)
                        if g:
                            s.wait_ge(VE.h, g)
                        r0 = ILOC * (c // 2) + 128 * it
                        LD.inc(s.dma_start(
                            out=xcm[it][c % 2][:],
                            in_=xcm_dram[r0:r0 + 128,
                                         TC * (c % 2):TC * (c % 2 + 1)]), 16)
                    ld_xcm[it][c % 2] = LD.n

                blk = c // 2
                col0 = (c % 2) * TC

                @on("sync")
                def _(s, c=c, blk=blk, col0=col0, g=g_dbw[c % 2]):
                    s.wait_ge(CC.h, n_cc_p1)
                    if g:
                        s.wait_ge(PE.h, g)
                    LD.inc(s.dma_start(out=dtw_w[c % 2][:],
                                       in_=cout_db[blk, 0:DTR, col0:col0 + TC]), 16)
                    LD.inc(s.dma_start(out=b_w[c % 2][:],
                                       in_=cout_db[blk, DTR:DTR + DS, col0:col0 + TC]),
                           16)
                    LD.inc(s.dma_start(out=c_w[c % 2][:],
                                       in_=cout_db[blk, DTR + DS:96, col0:col0 + TC]),
                           16)
                ld_dbw[c % 2] = LD.n

            xcm_load(0)

            se_dt_gate = 0
            g_spl = 0
            st_yf = 0
            ve_scan_gate = {}
            pe_ssum_gate = 0
            ve_hc_gate = 0
            ve_dub_gate = 0
            se_dA = {}
            ve_duB = {}
            ve_scan = {}
            ve_hC = {}
            pe_B = {}
            pe_C = {}
            n_ve_stt = 0

            for c in range(NCH):
                if c + 1 < NCH:
                    xcm_load(c + 1)

                for it in range(2):
                    @on("tensor")
                    def _(p, c=c, it=it, g=se_dt_gate):
                        W2(p, "tensor")
                        p.wait_ge(LD.h, ld_dbw[c % 2])
                        if g:
                            p.wait_ge(SE.h, g)
                        for nb in range(2):
                            li = p.matmul(ps3[:, 512 * nb:512 * (nb + 1)],
                                     dtwT_s[:, 128 * it:128 * (it + 1)],
                                     dtw_w[c % 2][:, 512 * nb:512 * (nb + 1)],
                                     start=True, stop=True)
                        PE.inc(li)
                    n_pe_dt = PE.n

                    @on("scalar")
                    def _(sc, c=c, it=it, gs=g_spl):
                        W2(sc, "scalar")
                        sc.wait_ge(PE.h, n_pe_dt)
                        if gs:
                            sc.wait_ge(VE.h, gs)
                        li = sc.activation(spl_a[:], ps3[:], AF.Abs,
                                      bias=dtbn[:, it:it + 1], scale=1.0)
                        li = sc.activation(spl_r[:], ps3[:], AF.Relu,
                                      bias=dtbn[:, it:it + 1], scale=1.0)
                        li = sc.activation(spl_a[:], spl_a[:], AF.Exp, bias=0.0, scale=-1.0)
                        SE.inc(sc.activation(spl_a[:], spl_a[:], AF.Ln,
                                             bias=one_t[:], scale=1.0))
                    se_dt_gate = SE.n
                    n_se_sp = SE.n

                    @on("vector")
                    def _(v, c=c, it=it):
                        W2(v, "vector")
                        v.wait_ge(SE.h, n_se_sp)
                        v.wait_ge(LD.h, ld_xcm[it][c % 2])
                        li = v.tensor_add(dcp[it][:], spl_r[:], spl_a[:])
                        li = v.tensor_single_scalar(dcp[it][:], dcp[it][:], 1.0, ALU.min)
                        VE.inc(v.tensor_mul(du_t[it][:], dcp[it][:],
                                            xcm[it][c % 2][:].bitcast(F32)))
                    n_ve_du = VE.n
                    g_spl = VE.n

                for s_i in range(DS):
                    par = s_i % 2

                    @on("tensor")
                    def _(p, c=c, s_i=s_i, g1=ve_hc_gate, g2=n_ve_stt):
                        if g1:
                            p.wait_ge(VE.h, g1)
                        for nb in range(2):
                            li = p.matmul(ps0[:, 512 * nb:512 * (nb + 1)],
                                     sel_s[:, 128 * s_i:128 * (s_i + 1)],
                                     b_w[c % 2][:, 512 * nb:512 * (nb + 1)],
                                     start=True, stop=True)
                        PE.inc(li)
                    pe_B[s_i] = PE.n

                    for it in range(2):
                        @on("scalar")
                        def _(sc, c=c, s_i=s_i, it=it, par=par,
                              g=ve_scan_gate.get((it, par), 0)):
                            if g:
                                sc.wait_ge(VE.h, g)
                            sc.wait_ge(VE.h, n_ve_du)
                            SE.inc(sc.activation(dab[it][par][:], dcp[it][:], AF.Exp,
                                                 bias=0.0,
                                                 scale=aneg[:, DS * it + s_i:DS * it + s_i + 1]))
                        se_dA[(s_i, it)] = SE.n

                    for it in range(2):
                        @on("vector")
                        def _(v, c=c, s_i=s_i, it=it, par=par,
                              g=ve_scan_gate.get((it, par), 0)):
                            v.wait_ge(PE.h, pe_B[s_i])
                            VE.inc(v.tensor_tensor(dub[it][par][:], du_t[it][:],
                                                   ps0[:], ALU.mult))
                        ve_duB[(s_i, it)] = VE.n
                    ve_dub_gate = VE.n

                    for it in range(2):
                        @on("vector")
                        def _(v, c=c, s_i=s_i, it=it, par=par):
                            v.wait_ge(SE.h, se_dA[(s_i, it)])
                            init = 0.0 if c == 0 else hlast[it][:, s_i:s_i + 1]
                            VE.inc(v.tensor_tensor_scan(h_buf[it][:], dab[it][par][:],
                                                 dub[it][par][:], init,
                                                 ALU.mult, ALU.add))
                        ve_scan[(s_i, it)] = VE.n
                        ve_scan_gate[(it, par)] = VE.n

                    @on("tensor")
                    def _(p, c=c, s_i=s_i):
                        p.wait_ge(VE.h, ve_dub_gate)
                        for nb in range(2):
                            li = p.matmul(ps0[:, 512 * nb:512 * (nb + 1)],
                                     sel_s[:, 128 * s_i:128 * (s_i + 1)],
                                     c_w[c % 2][:, 512 * nb:512 * (nb + 1)],
                                     start=True, stop=True)
                        PE.inc(li)
                    pe_C[s_i] = PE.n

                    for it in range(2):
                        @on("vector")
                        def _(v, s_i=s_i, it=it):
                            v.wait_ge(PE.h, pe_C[s_i])
                            VE.inc(v.tensor_tensor(hcr[it][:], h_buf[it][:],
                                                   ps0[:], ALU.mult))
                        ve_hC[(s_i, it)] = VE.n
                    ve_hc_gate = VE.n

                    @on("vector")
                    def _(v, c=c, s_i=s_i):
                        for it in range(2):
                            li = v.tensor_copy(hlast[it][:, s_i:s_i + 1],
                                               h_buf[it][:, TC - 1:TC])
                        VE.inc(li)

                    for it in range(2):
                        @on("tensor")
                        def _(p, c=c, s_i=s_i, it=it, g=pe_ssum_gate):
                            p.wait_ge(VE.h, ve_hC[(s_i, it)])
                            if s_i == 0 and g:
                                p.wait_ge(VE.h, g)
                            tgt = ps1 if it == 0 else ps2
                            for nb in range(2):
                                li = p.matmul(tgt[:, 512 * nb:512 * (nb + 1)],
                                         eye_rs[:],
                                         hcr[it][:, 512 * nb:512 * (nb + 1)],
                                         start=(s_i == 0), stop=(s_i == DS - 1))
                            PE.inc(li)
                n_pe_ssum = PE.n
                g_dbw[c % 2] = PE.n

                for it in range(2):
                    @on("vector")
                    def _(v, c=c, it=it, gy=st_yf):
                        v.wait_ge(PE.h, n_pe_ssum)
                        if gy:
                            v.wait_ge(ST.h, gy)
                        tgt = ps1 if it == 0 else ps2
                        VE.inc(v.scalar_tensor_tensor(
                            yfull[it][:], xcm[it][c % 2][:].bitcast(F32),
                            dpar_s[:, it:it + 1], tgt[:], ALU.mult, ALU.add))
                    g_xcm[it][c % 2] = VE.n
                n_ve_stt = VE.n
                pe_ssum_gate = VE.n

                for it in range(2):
                    @on("sync")
                    def _(s, c=c, it=it):
                        s.wait_ge(VE.h, n_ve_stt)
                        ST.inc(s.dma_start(
                            out=cin_y[ILOC * c + 128 * it:
                                      ILOC * c + 128 * (it + 1), :],
                            in_=yfull[it][:]), 16)
                st_yf = ST.n

                @on("gpsimd")
                def _(g, c=c, st=ST.n):
                    g.wait_ge(ST.h, st)
                    CC.inc(g.collective_compute(
                        "AllGather", ALU.bypass, rg,
                        [cin_y[ILOC * c:ILOC * (c + 1), :].opt()],
                        [cout_y[NCORE * ILOC * c:
                                NCORE * ILOC * (c + 1), :].opt()]))
            n_st_y = ST.n
            n_cc_y = CC.n

            if "y" in dbg_outs:
                @on("sync")
                def _(s):
                    s.wait_ge(CC.h, n_cc_y)
                    ST.inc(s.dma_start(out=dbg_outs["y"], in_=cin_y), 16)
            b2 = barrier_vals()

        if maxphase < 3:
            @on("sync")
            def _(s):
                wait_all(s, b2)
                s.wait_ge(ST.h, ST.n)
            return nc, dbg_outs

        with ExitStack() as st3:
            e3 = st3.enter_context

            def sb3(name, shp, dt):
                return e3(nc.sbuf_tensor(name, shp, dt))

            wo_s = sb3("wo_s", [128, 16 * D], F32R)
            y_t = [sb3(f"y_t{i}", [128, TLOC], F32) for i in range(2)]
            z_t = [sb3(f"z_t{i}", [128, TLOC], F32) for i in range(2)]
            silu_t = sb3("silu_t", [128, TLOC], F32)
            g_t = sb3("g_t", [128, TLOC], F32R)
            g_in = [sb3(f"g_in{i}", [128, 512], F32R) for i in range(2)]
            mo_nb = [sb3(f"mo_nb{i}", [128, 512], F32) for i in range(8)]
            mot_t = sb3("mot_t", [128, D], F32)
            mop_t = sb3("mop_t", [128, D], F32)
            xn3_t = sb3("xn3_t", [128, D], F32)
            x3_t = sb3("x3_t", [128, D], F32)
            x1_t = sb3("x1_t", [128, D], F32)

            fst3 = {k: True for k in ("sync", "vector", "scalar", "tensor")}

            def W3(eng, name):
                if fst3[name]:
                    wait_all(eng, b2)
                    fst3[name] = False

            @on("sync")
            def _(s):
                W3(s, "sync")
                for k in range(16):
                    LD.inc(s.dma_start(out=wo_s[:, D * k:D * (k + 1)],
                                       in_=woT[128 * k:128 * (k + 1), :]), 16)

            @on("sync")
            def _(s):
                W3(s, "sync")
                s.wait_ge(CC.h, n_cc_y)
                rowb = nc.snap(RuntimeValue(r_cid) * (2 * NCORE * ILOC))
                for dc in range(2):
                    ST.inc(s.dma_start(
                        out=ym_dram[:, TC * dc:TC * (dc + 1)],
                        in_=cout_y[bass.DynSlice(
                            rowb + NCORE * ILOC * dc, NCORE * ILOC), :]), 16)
            n_st_yslice = ST.n

            g_yz = [0, 0]
            g_silu = 0
            g_gst = 0
            pe_gmm = {}
            tt_gate_tr = 0
            st_out = 0
            for m in range(16):

                @on("sync")
                def _(s, m=m, g=g_yz[m % 2]):
                    W3(s, "sync")
                    s.wait_ge(ST.h, n_st_yslice)
                    if g:
                        s.wait_ge(VE.h, g)
                    LD.inc(s.dma_start(out=y_t[m % 2][:],
                                       in_=ym_dram[128 * m:128 * (m + 1), :]), 16)
                    LD.inc(s.dma_start(out=z_t[m % 2][:],
                                       in_=z_park[128 * m:128 * (m + 1), :]), 16)
                n_ld_yz = LD.n

                @on("scalar")
                def _(sc, m=m):
                    W3(sc, "scalar")
                    sc.wait_ge(LD.h, n_ld_yz)
                    if m >= 1:
                        sc.wait_ge(VE.h, g_silu)
                    SE.inc(sc.activation(silu_t[:], z_t[m % 2][:], AF.Silu,
                                         bias=0.0, scale=1.0))
                n_se_silu = SE.n

                @on("vector")
                def _(v, m=m):
                    W3(v, "vector")
                    v.wait_ge(SE.h, n_se_silu)
                    if m >= 1:
                        v.wait_ge(ST.h, g_gst)
                    VE.inc(v.tensor_mul(g_t[:], y_t[m % 2][:], silu_t[:]))
                g_yz[m % 2] = VE.n
                g_silu = VE.n

                @on("sync")
                def _(s, m=m):
                    s.wait_ge(VE.h, g_silu)
                    ST.inc(s.dma_start(out=g_park[128 * m:128 * (m + 1), :],
                                       in_=g_t[:]), 16)
                g_gst = ST.n
            n_st_g = ST.n

            se_monb = 0
            for nb in range(4):
                ld_g = {}
                for k in range(16):
                    @on("sync")
                    def _(s, nb=nb, k=k):
                        s.wait_ge(ST.h, n_st_g)
                        gk = pe_gmm.get((nb, k - 2)) if k >= 2 else (
                            pe_gmm.get((nb - 1, 14 + k)) if nb >= 1 else None)
                        if gk:
                            s.wait_ge(PE.h, gk)
                        LD.inc(s.dma_start(
                            out=g_in[k % 2][:],
                            in_=g_park[128 * k:128 * (k + 1),
                                       512 * nb:512 * (nb + 1)]), 16)
                    ld_g[k] = LD.n

                    @on("tensor")
                    def _(p, nb=nb, k=k, g=tt_gate_tr):
                        W3(p, "tensor")
                        p.wait_ge(LD.h, ld_g[k])
                        if k == 0 and g:
                            p.wait_ge(SE.h, g)
                        for mc in range(8):
                            tgt = [ps0, ps1, ps2, ps3][mc // 2]
                            ph = tgt[:, 512 * (mc % 2):512 * (mc % 2) + 512]
                            li = p.matmul(ph, wo_s[:, D * k + 128 * mc:D * k + 128 * (mc + 1)],
                                     g_in[k % 2][:], start=(k == 0), stop=(k == 15))
                        PE.inc(li)
                    pe_gmm[(nb, k)] = PE.n
                n_pe_mo = PE.n

                @on("scalar")
                def _(sc, nb=nb):
                    sc.wait_ge(PE.h, n_pe_mo)
                    for mc in range(8):
                        tgt = [ps0, ps1, ps2, ps3][mc // 2]
                        ph = tgt[:, 512 * (mc % 2):512 * (mc % 2) + 512]
                        li = sc.activation(mo_nb[mc][:], ph, AF.Copy, bias=0.0, scale=1.0)
                    SE.inc(li)
                se_monb = SE.n

                for sub in range(4):
                    tt = nb * 4 + sub

                    @on("tensor")
                    def _(p, nb=nb, sub=sub, g=se_monb, g2=tt_gate_tr):
                        p.wait_ge(SE.h, g)
                        if g2:
                            p.wait_ge(SE.h, g2)
                        for mc in range(8):
                            li = p.transpose(ps0[:, 128 * mc:128 * (mc + 1)],
                                        mo_nb[mc][:, 128 * sub:128 * (sub + 1)],
                                        eye_fs[:])
                        PE.inc(li)
                    n_pe_mtr = PE.n

                    @on("scalar")
                    def _(sc):
                        sc.wait_ge(PE.h, n_pe_mtr)
                        for mc in range(8):
                            li = sc.activation(mot_t[:, 128 * mc:128 * (mc + 1)],
                                          ps0[:, 128 * mc:128 * (mc + 1)],
                                          AF.Copy, bias=0.0, scale=1.0)
                        SE.inc(li)
                    tt_gate_tr = SE.n
                    n_se_mot = SE.n

                    @on("sync")
                    def _(s, tt=tt):
                        LD.inc(s.dma_start(out=xn3_t[:],
                                           in_=xn_park[128 * tt:128 * (tt + 1), :]), 16)
                        LD.inc(s.dma_start(
                            out=x3_t[:],
                            in_=x_ext[HALO + 128 * tt:HALO + 128 * (tt + 1), :]), 16)
                    n_ld_xn3 = LD.n

                    @on("vector")
                    def _(v):
                        v.wait_ge(SE.h, n_se_mot)
                        v.wait_ge(LD.h, n_ld_xn3)
                        VE.inc(v.tensor_add(mop_t[:], mot_t[:], xn3_t[:]))
                    n_ve_mop = VE.n

                    @on("scalar")
                    def _(sc):
                        sc.wait_ge(VE.h, n_ve_mop)
                        li = sc.activation(sdump[:, 0:D], mop_t[:], AF.Copy, bias=0.0,
                                      scale=1.0, accum_out=scr[:, 0:1])
                        SE.inc(sc.activation(sdump[:, 0:D], mop_t[:], AF.Square,
                                             bias=0.0, scale=1.0,
                                             accum_out=scr[:, 1:2]))
                    n_se_st3 = SE.n

                    @on("vector")
                    def _(v):
                        v.wait_ge(SE.h, n_se_st3)
                        li = v.tensor_scalar_mul(scr[:, 2:3], scr[:, 0:1], -1.0 / D)
                        li = v.tensor_scalar_mul(scr[:, 3:4], scr[:, 1:2], 1.0 / D)
                        li = v.tensor_mul(scr[:, 4:5], scr[:, 2:3], scr[:, 2:3])
                        v.tensor_max(vspB[:, 0:256], vspA[:, 0:256], vspA[:, 0:256])
                        VE.inc(v.tensor_sub(scr[:, 5:6], scr[:, 3:4], scr[:, 4:5]))
                    n_ve_v3 = VE.n

                    @on("scalar")
                    def _(sc):
                        sc.wait_ge(VE.h, n_ve_v3)
                        SE.inc(sc.activation(scr[:, 6:7], scr[:, 5:6], AF.Sqrt,
                                             bias=eps1_t[:], scale=1.0))
                    n_se_sq3 = SE.n

                    @on("vector")
                    def _(v, tt=tt, g=st_out):
                        v.wait_ge(SE.h, n_se_sq3)
                        li = v.reciprocal(scr[:, 7:8], scr[:, 6:7])
                        v.tensor_max(vspB[:, 0:256], vspA[:, 0:256], vspA[:, 0:256])
                        li = v.tensor_mul(scr[:, 8:9], scr[:, 2:3], scr[:, 7:8])
                        if g:
                            v.wait_ge(ST.h, g)
                        v.tensor_max(vspB[:, 0:256], vspA[:, 0:256], vspA[:, 0:256])
                        li = v.tensor_scalar(x1_t[:], mop_t[:], scr[:, 7:8], scr[:, 8:9],
                                        ALU.mult, ALU.add)
                        VE.inc(v.tensor_add(x1_t[:], x1_t[:], x3_t[:]))
                    n_ve_x1 = VE.n

                    @on("sync")
                    def _(s, tt=tt):
                        s.wait_ge(VE.h, n_ve_x1)
                        ST.inc(s.dma_start(out=x1_park[128 * tt:128 * (tt + 1), :],
                                           in_=x1_t[:]), 16)
                    st_out = ST.n
            n_st_x1 = ST.n

            if "x1" in dbg_outs:
                @on("sync")
                def _(s):
                    s.wait_ge(ST.h, n_st_x1)
                    ST.inc(s.dma_start(out=dbg_outs["x1"], in_=x1_park), 16)
            b3 = barrier_vals()

        if maxphase < 4:
            @on("sync")
            def _(s):
                wait_all(s, b3)
                s.wait_ge(ST.h, ST.n)
            return nc, dbg_outs

        with ExitStack() as st4:
            e4 = st4.enter_context

            def sb4(name, shp, dt):
                return e4(nc.sbuf_tensor(name, shp, dt))

            xqT = sb4("xqT", [128, T], BF16)
            x4_t = [sb4(f"x4_t{i}", [128, D], F32) for i in range(2)]
            q_i32 = sb4("q_i32", [128, D], I32)
            q_bf = sb4("q_bf", [128, D], F32)
            gumax = sb4("gumax", [128, 16], F32)

            fst4 = {k: True for k in ("sync", "vector", "scalar", "tensor")}

            def W4(eng, name):
                if fst4[name]:
                    wait_all(eng, b3)
                    fst4[name] = False

            x4n = sb4("x4n", [128, D], F32)
            g_x4 = [0, 0]
            g_qtr = 0
            g_sdq = 0
            g_qbf = 0
            for tt in range(16):
                @on("sync")
                def _(s, tt=tt, g=g_x4[tt % 2]):
                    W4(s, "sync")
                    if g:
                        s.wait_ge(VE.h, g)
                    LD.inc(s.dma_start(out=x4_t[tt % 2][:],
                                       in_=x1_park[128 * tt:128 * (tt + 1), :]), 16)
                n_ld_x4 = LD.n

                @on("scalar")
                def _(sc, tt=tt):
                    W4(sc, "scalar")
                    sc.wait_ge(LD.h, n_ld_x4)
                    li = sc.activation(sdump[:, 0:D], x4_t[tt % 2][:], AF.Copy,
                                  bias=0.0, scale=1.0, accum_out=scr[:, 4:5])
                    SE.inc(sc.activation(sdump[:, 0:D], x4_t[tt % 2][:], AF.Square,
                                         bias=0.0, scale=1.0,
                                         accum_out=scr[:, 5:6]))
                n_se_st4p = SE.n

                @on("vector")
                def _(v, tt=tt):
                    W4(v, "vector")
                    v.wait_ge(SE.h, n_se_st4p)
                    li = v.tensor_scalar_mul(scr[:, 6:7], scr[:, 4:5], -1.0 / D)
                    li = v.tensor_scalar_mul(scr[:, 7:8], scr[:, 5:6], 1.0 / D)
                    li = v.tensor_mul(scr[:, 8:9], scr[:, 6:7], scr[:, 6:7])
                    v.tensor_max(vspB[:, 0:256], vspA[:, 0:256], vspA[:, 0:256])
                    VE.inc(v.tensor_sub(scr[:, 9:10], scr[:, 7:8], scr[:, 8:9]))
                n_ve_var4 = VE.n

                @on("scalar")
                def _(sc, tt=tt):
                    sc.wait_ge(VE.h, n_ve_var4)
                    SE.inc(sc.activation(scr[:, 10:11], scr[:, 9:10], AF.Sqrt,
                                         bias=eps1_t[:], scale=1.0))
                n_se_sq4p = SE.n

                @on("vector")
                def _(v, tt=tt):
                    v.wait_ge(SE.h, n_se_sq4p)
                    li = v.reciprocal(scr[:, 11:12], scr[:, 10:11])
                    v.tensor_max(vspB[:, 0:256], vspA[:, 0:256], vspA[:, 0:256])
                    li = v.tensor_mul(scr[:, 12:13], scr[:, 6:7], scr[:, 11:12])
                    v.tensor_max(vspB[:, 0:256], vspA[:, 0:256], vspA[:, 0:256])
                    li = v.tensor_scalar(x4n[:], x4_t[tt % 2][:], scr[:, 11:12],
                                    scr[:, 12:13], ALU.mult, ALU.add)
                    li = v.tensor_reduce(scr[:, 0:1], x4n[:], mybir.AxisListType.X,
                                    ALU.max, apply_absolute_value=True)
                    li = v.tensor_max(vspB[:], vspA[:], vspA[:])
                    li = v.tensor_single_scalar(scr[:, 0:1], scr[:, 0:1], 1e-5, ALU.max)
                    v.tensor_max(vspB[:, 0:256], vspA[:, 0:256], vspA[:, 0:256])
                    li = v.reciprocal(scr[:, 1:2], scr[:, 0:1])
                    v.tensor_max(vspB[:, 0:256], vspA[:, 0:256], vspA[:, 0:256])
                    li = v.tensor_scalar_mul(sxf[:, tt:tt + 1], scr[:, 1:2], 127.0)
                    li = v.tensor_scalar_mul(scr[:, 2:3], scr[:, 0:1], 1.0 / 127.0)
                    v.tensor_max(vspB[:, 0:256], vspA[:, 0:256], vspA[:, 0:256])
                    li = v.tensor_mul(sxf[:, 16 + tt:17 + tt], scr[:, 2:3], sxf[:, 61:62])
                    VE.inc(v.tensor_mul(sxf[:, 32 + tt:33 + tt], scr[:, 2:3],
                                        sxf[:, 62:63]))
                n_ve_aq = VE.n
                g_x4[tt % 2] = VE.n

                @on("scalar")
                def _(sc, tt=tt, g=g_sdq):
                    sc.wait_ge(VE.h, n_ve_aq)
                    if g:
                        sc.wait_ge(VE.h, g)
                    SE.inc(sc.activation(sdump[:, 0:D], x4n[:], AF.Copy,
                                         bias=0.0, scale=sxf[:, tt:tt + 1]))
                n_se_q = SE.n

                @on("vector")
                def _(v, tt=tt, g=g_qbf):
                    v.wait_ge(SE.h, n_se_q)
                    if g:
                        v.wait_ge(PE.h, g)
                    li = v.tensor_copy(q_i32[:], sdump[:, 0:D])
                    VE.inc(v.tensor_copy(q_bf[:], q_i32[:]))
                n_ve_q = VE.n
                g_sdq = VE.n

                @on("tensor")
                def _(p, tt=tt, g=g_qtr):
                    W4(p, "tensor")
                    p.wait_ge(VE.h, n_ve_q)
                    if g:
                        p.wait_ge(SE.h, g)
                    for j in range(8):
                        li = p.transpose(ps3[:, 128 * j:128 * (j + 1)],
                                    q_bf[:, 128 * j:128 * (j + 1)], eye_fs[:])
                    PE.inc(li)
                n_pe_qtr = PE.n
                g_qbf = PE.n

                @on("scalar")
                def _(sc, tt=tt):
                    sc.wait_ge(PE.h, n_pe_qtr)
                    for j in range(8):
                        li = sc.activation(xqT[:, D * tt + 128 * j:D * tt + 128 * (j + 1)],
                                      ps3[:, 128 * j:128 * (j + 1)],
                                      AF.Copy, bias=0.0, scale=1.0)
                    SE.inc(li)
                g_qtr = SE.n
            n_se_xqT = SE.n

            if "xqt" in dbg_outs:
                @on("sync")
                def _(s):
                    s.wait_ge(SE.h, n_se_xqT)
                    ST.inc(s.dma_start(out=dbg_outs["xqt"], in_=xqT[:]), 16)
            if "sxf" in dbg_outs:
                @on("sync")
                def _(s):
                    s.wait_ge(SE.h, n_se_xqT)
                    ST.inc(s.dma_start(out=dbg_outs["sxf"], in_=sxf[:]), 16)

            with ExitStack() as st4a:
                e4a = st4a.enter_context
                wgb = [e4a(nc.sbuf_tensor(f"wgb{i}", [128, 8 * 1024], BF16))
                       for i in range(2)]
                wub = [e4a(nc.sbuf_tensor(f"wub{i}", [128, 8 * 1024], BF16))
                       for i in range(2)]
                gsil = e4a(nc.sbuf_tensor("gsil", [128, 1024], F32))
                gu_t = e4a(nc.sbuf_tensor("gu_t", [128, 1024], F32))

                ld_wb = {}
                pe_wb_use = {}
                for ib in range(4):
                    @on("sync")
                    def _(s, ib=ib):
                        s.wait_ge(CC.h, CC.n)
                        if ib >= 2:
                            s.wait_ge(PE.h, pe_wb_use.get(ib - 2, 0))
                        for k in range(8):
                            LD.inc(s.dma_start(
                                out=wgb[ib % 2][:, 1024 * k:1024 * (k + 1)],
                                in_=cout_wg[k, :, 1024 * ib:1024 * (ib + 1)]), 16)
                            LD.inc(s.dma_start(
                                out=wub[ib % 2][:, 1024 * k:1024 * (k + 1)],
                                in_=cout_wu[k, :, 1024 * ib:1024 * (ib + 1)]), 16)
                    ld_wb[ib] = LD.n

                    g_guev = 0
                    for tt in range(16):
                        @on("tensor")
                        def _(p, ib=ib, tt=tt, g=g_guev):
                            p.wait_ge(LD.h, ld_wb[ib])
                            p.wait_ge(SE.h, n_se_xqT)
                            if g:
                                p.wait_ge(VE.h, g)
                            for half in range(2):
                                for k in range(8):
                                    lhs = xqT[:, D * tt + 128 * k:D * tt + 128 * (k + 1)]
                                    li = p.matmul(ps0[:, 512 * half:512 * (half + 1)], lhs,
                                             wgb[ib % 2][:, 1024 * k + 512 * half:
                                                          1024 * k + 512 * (half + 1)],
                                             start=(k == 0), stop=(k == 7))
                            for half in range(2):
                                for k in range(8):
                                    lhs = xqT[:, D * tt + 128 * k:D * tt + 128 * (k + 1)]
                                    li = p.matmul(ps1[:, 512 * half:512 * (half + 1)], lhs,
                                             wub[ib % 2][:, 1024 * k + 512 * half:
                                                          1024 * k + 512 * (half + 1)],
                                             start=(k == 0), stop=(k == 7))
                            PE.inc(li)
                        n_pe_gu = PE.n
                        pe_wb_use[ib] = PE.n

                        @on("scalar")
                        def _(sc, ib=ib, tt=tt):
                            sc.wait_ge(PE.h, n_pe_gu)
                            if True:
                                sc.wait_ge(VE.h, VE.n)
                            SE.inc(sc.activation(gsil[:], ps0[:], AF.Silu,
                                                 bias=0.0,
                                                 scale=sxf[:, 16 + tt:17 + tt]))
                        n_se_gs = SE.n

                        @on("vector")
                        def _(v, ib=ib, tt=tt):
                            v.wait_ge(SE.h, n_se_gs)
                            if True:
                                v.wait_ge(ST.h, ST.n)
                            li = v.scalar_tensor_tensor(gu_t[:], ps1[:],
                                                   sxf[:, 32 + tt:33 + tt], gsil[:],
                                                   ALU.mult, ALU.mult)
                            li = v.tensor_reduce(scr[:, 0:1], gu_t[:], mybir.AxisListType.X,
                                            ALU.max, apply_absolute_value=True)
                            li = v.tensor_max(vspB[:], vspA[:], vspA[:])
                            if ib == 0:
                                li = v.tensor_copy(gumax[:, tt:tt + 1], scr[:, 0:1])
                            else:
                                li = v.tensor_max(gumax[:, tt:tt + 1], gumax[:, tt:tt + 1],
                                             scr[:, 0:1])
                            VE.inc(li)
                        n_ve_gu = VE.n
                        g_guev = VE.n

                        @on("sync")
                        def _(s, ib=ib, tt=tt):
                            s.wait_ge(VE.h, n_ve_gu)
                            ST.inc(s.dma_start(
                                out=gu_park[128 * tt:128 * (tt + 1),
                                            1024 * ib:1024 * (ib + 1)],
                                in_=gu_t[:]), 16)
                n_st_gu = ST.n
                if "gu" in dbg_outs:
                    @on("sync")
                    def _(s):
                        s.wait_ge(ST.h, n_st_gu)
                        ST.inc(s.dma_start(out=dbg_outs["gu"], in_=gu_park), 16)
                b4a = barrier_vals()

            with ExitStack() as st4b:
                e4b = st4b.enter_context
                wdq_sb = e4b(nc.sbuf_tensor("wdq_sb", [128, 32 * 1024], BF16))
                guqT = e4b(nc.sbuf_tensor("guqT", [128, 4096], BF16))
                gu4 = [e4b(nc.sbuf_tensor(f"gu4_{i}", [128, 1024], F32))
                       for i in range(2)]
                ffo = e4b(nc.sbuf_tensor("ffo", [128, D], F32))
                outb = e4b(nc.sbuf_tensor("outb", [128, D], F32))

                fst4b = {k: True for k in ("sync", "vector", "scalar", "tensor")}

                def W4b(eng, name):
                    if fst4b[name]:
                        wait_all(eng, b4a)
                        fst4b[name] = False

                @on("sync")
                def _(s):
                    W4b(s, "sync")
                    for kk in range(32):
                        LD.inc(s.dma_start(
                            out=wdq_sb[:, 1024 * kk:1024 * (kk + 1)],
                            in_=cout_wd[kk // 4, 128 * (kk % 4):128 * (kk % 4) + 128, :]),
                            16)
                n_ld_wd = LD.n

                @on("vector")
                def _(v):
                    W4b(v, "vector")
                    li = v.tensor_single_scalar(gumax[:], gumax[:], 1e-5, ALU.max)
                    v.tensor_max(vspB[:, 0:256], vspA[:, 0:256], vspA[:, 0:256])
                    li = v.reciprocal(scr[:, 0:16], gumax[:])
                    v.tensor_max(vspB[:, 0:256], vspA[:, 0:256], vspA[:, 0:256])
                    li = v.tensor_scalar_mul(sxf[:, 0:16], scr[:, 0:16], 127.0)
                    li = v.tensor_scalar_mul(scr[:, 0:16], gumax[:], 1.0 / 127.0)
                    v.tensor_max(vspB[:, 0:256], vspA[:, 0:256], vspA[:, 0:256])
                    for tt in range(16):
                        li = v.tensor_mul(sxf[:, 32 + tt:33 + tt], scr[:, tt:tt + 1],
                                     sxf[:, 63:64])
                    VE.inc(li)
                n_ve_s2 = VE.n

                g_dtr = 0
                g_gu4 = [0, 0]
                g_vq4 = 0
                g_pt4 = 0
                g_vout = 0
                for tt in range(16):
                    for ib in range(4):
                        @on("sync")
                        def _(s, tt=tt, ib=ib, g=g_gu4[ib % 2]):
                            W4b(s, "sync")
                            s.wait_ge(ST.h, n_st_gu)
                            if g:
                                s.wait_ge(SE.h, g)
                            LD.inc(s.dma_start(
                                out=gu4[ib % 2][:],
                                in_=gu_park[128 * tt:128 * (tt + 1),
                                            1024 * ib:1024 * (ib + 1)]), 16)
                        n_ld_gu4 = LD.n

                        @on("scalar")
                        def _(sc, tt=tt, ib=ib, gq=g_vq4):
                            W4b(sc, "scalar")
                            sc.wait_ge(LD.h, n_ld_gu4)
                            sc.wait_ge(VE.h, n_ve_s2)
                            if gq:
                                sc.wait_ge(VE.h, gq)
                            SE.inc(sc.activation(sdump[:, 0:1024], gu4[ib % 2][:],
                                                 AF.Copy, bias=0.0,
                                                 scale=sxf[:, tt:tt + 1]))
                        n_se_q4 = SE.n
                        g_gu4[ib % 2] = SE.n

                        @on("vector")
                        def _(v, tt=tt, ib=ib, gp=g_pt4):
                            v.wait_ge(SE.h, n_se_q4)
                            if gp:
                                v.wait_ge(PE.h, gp)
                            li = v.tensor_copy(q_i32[:, 0:1024], sdump[:, 0:1024])
                            VE.inc(v.tensor_copy(q_bf[:, 0:1024], q_i32[:, 0:1024]))
                        n_ve_q4 = VE.n
                        g_vq4 = VE.n

                        @on("tensor")
                        def _(p, tt=tt, ib=ib, g=g_dtr):
                            W4b(p, "tensor")
                            p.wait_ge(VE.h, n_ve_q4)
                            if g:
                                p.wait_ge(SE.h, g)
                            for j in range(8):
                                li = p.transpose(ps3[:, 128 * j:128 * (j + 1)],
                                            q_bf[:, 128 * j:128 * (j + 1)], eye_fs[:])
                            PE.inc(li)
                        n_pe_t4 = PE.n
                        g_pt4 = PE.n

                        @on("scalar")
                        def _(sc, tt=tt, ib=ib):
                            sc.wait_ge(PE.h, n_pe_t4)
                            for j in range(8):
                                kk = 8 * ib + j
                                li = sc.activation(guqT[:, 128 * kk:128 * (kk + 1)],
                                              ps3[:, 128 * j:128 * (j + 1)],
                                              AF.Copy, bias=0.0, scale=1.0)
                            SE.inc(li)
                        g_dtr = SE.n
                    n_se_guqT = SE.n

                    @on("tensor")
                    def _(p, tt=tt):
                        p.wait_ge(SE.h, n_se_guqT)
                        p.wait_ge(LD.h, n_ld_wd)
                        if tt >= 1:
                            p.wait_ge(SE.h, se_ffo)
                        for half in range(2):
                            for kk in range(32):
                                li = p.matmul(ps2[:, 512 * half:512 * (half + 1)],
                                         guqT[:, 128 * kk:128 * (kk + 1)],
                                         wdq_sb[:, 1024 * kk + 512 * half:
                                                 1024 * kk + 512 * (half + 1)],
                                         start=(kk == 0), stop=(kk == 31))
                        PE.inc(li)
                    n_pe_dn = PE.n

                    @on("scalar")
                    def _(sc, tt=tt, gv=g_vout):
                        sc.wait_ge(PE.h, n_pe_dn)
                        if gv:
                            sc.wait_ge(VE.h, gv)
                        li = sc.activation(ffo[:], ps2[:], AF.Copy, bias=0.0,
                                      scale=sxf[:, 32 + tt:33 + tt])
                        li = sc.activation(sdump[:, 0:D], ffo[:], AF.Copy, bias=0.0,
                                      scale=1.0, accum_out=scr[:, 16:17])
                        SE.inc(sc.activation(sdump[:, 0:D], ffo[:], AF.Square,
                                             bias=0.0, scale=1.0,
                                             accum_out=scr[:, 17:18]))
                    se_ffo = SE.n
                    n_se_f = SE.n

                    @on("vector")
                    def _(v):
                        v.wait_ge(SE.h, n_se_f)
                        li = v.tensor_scalar_mul(scr[:, 18:19], scr[:, 16:17], -1.0 / D)
                        li = v.tensor_scalar_mul(scr[:, 19:20], scr[:, 17:18], 1.0 / D)
                        li = v.tensor_mul(scr[:, 20:21], scr[:, 18:19], scr[:, 18:19])
                        v.tensor_max(vspB[:, 0:256], vspA[:, 0:256], vspA[:, 0:256])
                        VE.inc(v.tensor_sub(scr[:, 21:22], scr[:, 19:20],
                                            scr[:, 20:21]))
                    n_ve_v4 = VE.n

                    @on("scalar")
                    def _(sc):
                        sc.wait_ge(VE.h, n_ve_v4)
                        SE.inc(sc.activation(scr[:, 22:23], scr[:, 21:22], AF.Sqrt,
                                             bias=eps1_t[:], scale=1.0))
                    n_se_sq4 = SE.n

                    @on("sync")
                    def _(s, tt=tt, gv=(g_vout if tt >= 2 else 0)):
                        if gv:
                            s.wait_ge(VE.h, gv)
                        LD.inc(s.dma_start(out=x4_t[tt % 2][:],
                                           in_=x1_park[128 * tt:128 * (tt + 1), :]), 16)
                    n_ld_x1b = LD.n

                    @on("vector")
                    def _(v, tt=tt):
                        v.wait_ge(SE.h, n_se_sq4)
                        v.wait_ge(LD.h, n_ld_x1b)
                        if tt >= 1:
                            v.wait_ge(ST.h, st_out)
                        li = v.reciprocal(scr[:, 23:24], scr[:, 22:23])
                        v.tensor_max(vspB[:, 0:256], vspA[:, 0:256], vspA[:, 0:256])
                        li = v.tensor_mul(scr[:, 24:25], scr[:, 18:19], scr[:, 23:24])
                        v.tensor_max(vspB[:, 0:256], vspA[:, 0:256], vspA[:, 0:256])
                        li = v.tensor_scalar(outb[:], ffo[:], scr[:, 23:24], scr[:, 24:25],
                                        ALU.mult, ALU.add)
                        VE.inc(v.tensor_add(outb[:], outb[:], x4_t[tt % 2][:]))
                    n_ve_out = VE.n
                    g_vout = VE.n

                    @on("sync")
                    def _(s, tt=tt):
                        s.wait_ge(VE.h, n_ve_out)
                        ST.inc(s.dma_start(out=out[128 * tt:128 * (tt + 1), :],
                                           in_=outb[:]), 16)
                        if "ffn" in dbg_outs:
                            ST.inc(s.dma_start(
                                out=dbg_outs["ffn"][128 * tt:128 * (tt + 1), :],
                                in_=ffo[:]), 16)
                    st_out = ST.n

            @on("sync")
            def _(s):
                s.wait_ge(ST.h, ST.n)

    return nc, dbg_outs


def _host_layout(inputs, cid):
    x = np.asarray(inputs["L0_attn_x"])[0]
    in_proj_w = np.ascontiguousarray(inputs["L0_attn_in_proj_w"])
    conv_w = np.asarray(inputs["L0_attn_conv_w"])
    conv_b = np.asarray(inputs["L0_attn_conv_b"])
    x_proj_w = np.asarray(inputs["L0_attn_x_proj_w"])
    dt_proj_w = np.asarray(inputs["L0_attn_dt_proj_w"])
    dt_proj_b = np.asarray(inputs["L0_attn_dt_proj_b"])
    a_log = np.asarray(inputs["L0_attn_A_log"])
    d_par = np.asarray(inputs["L0_attn_D"])
    out_proj_w = np.asarray(inputs["L0_attn_out_proj_w"])
    norm_w = np.asarray(inputs["L0_attn_norm_w"])
    w_gate = np.asarray(inputs["L0_ffn_w_gate"])
    w_up = np.asarray(inputs["L0_ffn_w_up"])
    w_down = np.asarray(inputs["L0_ffn_w_down"])

    t0 = TLOC * cid
    x_ext = np.zeros((TEXT, D), np.float32)
    lo = max(0, t0 - HALO)
    x_ext[HALO - (t0 - lo):HALO + TLOC] = x[lo:t0 + TLOC]

    diag = np.zeros((16, 4, 128, 128), np.float32)
    for m in range(16):
        for k in range(4):
            np.fill_diagonal(diag[m, k], conv_w[128 * m:128 * (m + 1), 0, k])

    xprojT = np.zeros((128, 16 * 96), np.float32)
    xpT = np.ascontiguousarray(x_proj_w.T)
    for m in range(16):
        xprojT[:, 96 * m:96 * (m + 1)] = xpT[128 * m:128 * (m + 1), :]

    sel_h = np.zeros((DS, DS * 128), np.float32)
    for s_i in range(DS):
        sel_h[s_i, 128 * s_i:128 * (s_i + 1)] = 1.0

    ch0 = ILOC * cid
    im = {
        "x_ext": x_ext,
        "cid_in": np.array([[cid]], np.int32),
        "w1T": np.ascontiguousarray(in_proj_w.T),
        "conv_diag": diag,
        "conv_b": np.ascontiguousarray(conv_b.reshape(16, 128).T),
        "xprojT": xprojT,
        "dtwT": np.ascontiguousarray(dt_proj_w.T[:, ch0:ch0 + ILOC]),
        "dtb": np.ascontiguousarray(dt_proj_b[ch0:ch0 + ILOC].reshape(ILOC, 1)),
        "a_log": np.ascontiguousarray(a_log[ch0:ch0 + ILOC]),
        "d_par": np.ascontiguousarray(d_par[ch0:ch0 + ILOC].reshape(ILOC, 1)),
        "woT": np.ascontiguousarray(out_proj_w.T),
        "norm_w": np.ascontiguousarray(norm_w.reshape(8, 128).T),
        "eye_f": np.eye(128, dtype=np.float32),
        "eye_b": np.eye(128, dtype=np.float32),
        "eye_r": np.eye(128, dtype=np.float32),
        "sel": sel_h,
        "wgT_sh": np.ascontiguousarray(w_gate.T[128 * cid:128 * (cid + 1), :]),
        "wuT_sh": np.ascontiguousarray(w_up.T[128 * cid:128 * (cid + 1), :]),
        "wdT_sh": np.ascontiguousarray(w_down.T[512 * cid:512 * (cid + 1), :]),
    }
    import ml_dtypes
    im["eye_b"] = im["eye_b"].astype(ml_dtypes.bfloat16)
    return im


_CACHE = {}
XC_INLINE = False


def kernel(**inputs):
    key = (tuple(sorted(DEBUG)), MAXPHASE, P1CUT, tuple(sorted(SKIP)))
    if key not in _CACHE:
        _CACHE[key] = build(DEBUG, MAXPHASE, P1CUT, SKIP)
    nc, dbg_outs = _CACHE[key]
    in_maps = [_host_layout(inputs, c) for c in range(NCORE)]
    res = run_bass_kernel_spmd(nc, in_maps, core_ids=list(range(NCORE)))
    outp = np.concatenate([res.results[c]["out"] for c in range(NCORE)], axis=0)
    outp = outp.reshape(1, T, D)
    if DEBUG:
        dbgd = {name: [res.results[c]["dbg_" + name] for c in range(NCORE)]
                for name in dbg_outs}
        return outp, dbgd
    return outp
